# version 33
# baseline (speedup 1.0000x reference)
import contextlib
import numpy as np
import ml_dtypes
import concourse.bass as bass
import concourse.mybir as mybir
from concourse.bass_utils import run_bass_kernel_spmd

F32 = mybir.dt.float32
BF16 = mybir.dt.bfloat16
I32 = mybir.dt.int32
AF = mybir.ActivationFunctionType
ALU = mybir.AluOpType

D = 1024
D2 = 2048
L = 2048
TC = 512
NCH = L // TC
NDT = D2 // 128
ND = D // 128
NS = 16
RK = 64
DF = 4096
NF = DF // 128
H2 = 3072
NH = H2 // 128
QSCALE = 1.0 / 127.0
KB = 1024

ENGS = ("SP", "PE", "ACT", "DVE", "GP")
USE_DWAITS = False


class _Op:
    __slots__ = ("eng", "emit", "dma", "inc", "val", "deps", "seq", "bar",
                 "own_wait", "bar_target", "snapshot", "fsync", "dq", "skey",
                 "dwaits")

    def __init__(self):
        self.bar = False
        self.inc = False
        self.val = None
        self.deps = set()
        self.fsync = False
        self.dq = None


class _Buf:
    __slots__ = ("writes", "readers", "alias_prev")

    def __init__(self):
        self.writes = []
        self.readers = []
        self.alias_prev = []


DMA_LANES = {"SP": 4, "GP": 2, "ACT": 2, "DVE": 1, "PE": 1}


class Prog:
    def __init__(self):
        self.ops = []
        self.bufs = []
        self.nbar = 0
        self.last_per_engine = {e: None for e in ENGS}
        self.dma_count = {e: 0 for e in ENGS}
        self.lane_counts = {}

    def buf(self):
        b = _Buf()
        self.bufs.append(b)
        return b

    def _needs(self, d, o):
        if d is o:
            return False
        if d.fsync:
            return True
        if d.eng == o.eng:
            if d.dma and o.dma:
                return d.dq != o.dq
            if (not d.dma) and (not o.dma):
                return False
            return True
        return True

    def op(self, eng, emit, r=(), w=(), dma=False, extra=(), fsync=False,
           dq=None):
        o = _Op()
        o.eng = eng
        o.emit = emit
        o.dma = dma
        o.inc = dma
        o.fsync = fsync
        o.dq = dq if dq is not None else (eng if dma else None)
        o.dwaits = {}
        for d in o.deps:
            if d.dma:
                o.dwaits[d.skey] = self.lane_counts[d.skey]
        if dma:
            lane = self.dma_count[eng] % DMA_LANES[eng]
            self.dma_count[eng] += 1
            o.skey = (eng, lane)
            self.lane_counts[o.skey] = self.lane_counts.get(o.skey, 0) + 16
        else:
            o.skey = (eng, -1)
        for b in r:
            for d in b.writes:
                if self._needs(d, o):
                    o.deps.add(d)
        for b in w:
            for d in b.writes:
                if self._needs(d, o):
                    o.deps.add(d)
            for d in b.readers:
                if self._needs(d, o):
                    o.deps.add(d)
            for ob in b.alias_prev:
                for d in ob.writes + ob.readers:
                    if self._needs(d, o):
                        o.deps.add(d)
        for d in extra:
            if self._needs(d, o):
                o.deps.add(d)
        for d in o.deps:
            d.inc = True
        for b in r:
            b.readers.append(o)
        for b in w:
            b.writes = [o]
            b.readers = []
            b.alias_prev = []
        o.seq = len(self.ops)
        self.ops.append(o)
        self.last_per_engine[eng] = o
        return o

    def barrier(self):
        self.nbar += 1
        for e in ENGS:
            o = _Op()
            o.eng = e
            o.emit = None
            o.dma = False
            o.bar = True
            last = self.last_per_engine[e]
            if last is not None and not last.bar:
                last.inc = True
            o.own_wait = None
            o.bar_target = 5 * self.nbar
            o.seq = len(self.ops)
            self.ops.append(o)
            self.last_per_engine[e] = o
        for b in self.bufs:
            b.writes = []
            b.readers = []

    def assign_values(self):
        cnt = {}
        for o in self.ops:
            if o.bar:
                o.snapshot = dict(cnt)
            elif o.inc:
                cnt[o.skey] = cnt.get(o.skey, 0) + (16 if o.dma else 1)
                o.val = cnt[o.skey]

    def emit_engine(self, eng, engobj, sems, sbar):
        waited = {}
        for o in self.ops:
            if o.eng != eng:
                continue
            if o.bar:
                for k, v in o.snapshot.items():
                    if k[0] == eng and waited.get(k, 0) < v:
                        engobj.wait_ge(sems[k], v)
                        waited[k] = v
                engobj.sem_inc(sbar, 1)
                engobj.wait_ge(sbar, o.bar_target)
                for k, v in o.snapshot.items():
                    waited[k] = max(waited.get(k, 0), v)
                continue
            merged = dict(o.dwaits) if USE_DWAITS else {}
            for dp in o.deps:
                if USE_DWAITS and dp.dma:
                    continue
                assert dp.val is not None
                if merged.get(dp.skey, 0) < dp.val:
                    merged[dp.skey] = dp.val
            for s, v in merged.items():
                if waited.get(s, 0) < v:
                    engobj.wait_ge(sems[s], v)
                    waited[s] = v
            inst = o.emit(engobj)
            if o.inc:
                inst.then_inc(sems[o.skey], 16 if o.dma else 1)


class Alloc:

    def __init__(self, nc, P, name, shape, dtype, psum=False):
        self.cm = (nc.psum_tensor if psum else nc.sbuf_tensor)(name, shape, dtype)
        self.h = self.cm.__enter__()
        self.b = P.buf()


_DT_BYTES = {F32: 4, BF16: 2, I32: 4}


class Manual:

    seq = [0]
    registry = []

    def __init__(self, nc, P, mmap, name, shape, dtype, off, phases):
        nbytes = int(np.prod(shape[1:])) * _DT_BYTES[dtype]
        for (o2, n2, ph2, nm2) in mmap:
            if phases & ph2 and off < o2 + n2 and o2 < off + nbytes:
                raise AssertionError(
                    f"sbuf overlap {name}@[{off},{off+nbytes}) ph{phases} vs "
                    f"{nm2}@[{o2},{o2+n2}) ph{ph2}")
        mmap.append((off, nbytes, set(phases), name))
        Manual.seq[0] += 1
        self.h = nc.alloc_sbuf_tensor_at(
            f"{name}_{Manual.seq[0]}", shape, dtype, offset=off)
        self.b = P.buf()
        lo, hi = off, off + nbytes
        keep = []
        for (l2, h2, b2) in Manual.registry:
            if l2 < hi and lo < h2:
                self.b.alias_prev.append(b2)
                if l2 >= lo and h2 <= hi:
                    continue
            keep.append((l2, h2, b2))
        keep.append((lo, hi, self.b))
        Manual.registry = keep
        self._lo, self._hi, self._P = lo, hi, P

    def split(self, n):
        bufs = [self._P.buf() for _ in range(n)]
        sz = (self._hi - self._lo) // n
        Manual.registry = [e for e in Manual.registry if e[2] is not self.b]
        for i, b in enumerate(bufs):
            b.alias_prev = list(self.b.alias_prev)
            Manual.registry.append(
                (self._lo + i * sz, self._lo + (i + 1) * sz, b))
        return bufs


def build_nc(debug=False):
    nc = bass.Bass("TRN2", target_bir_lowering=False, debug=False)
    P = Prog()
    Manual.registry = []

    d_in = {}

    def din(name, shape, dt):
        d_in[name] = nc.dram_tensor(name, shape, dt, kind="ExternalInput").ap()
        return d_in[name]

    in_x = din("in_x", [L, D], F32)
    w_ip = din("w_ip", [2 * D2, D], F32)
    w_cw = din("w_cw", [D2, 3], F32)
    w_cb = din("w_cb", [D2], F32)
    w_xp = din("w_xp", [RK + 2 * NS, D2], F32)
    w_dt = din("w_dt", [D2, RK], F32)
    w_dtb = din("w_dtb", [D2], F32)
    w_al = din("w_al", [D2, NS], F32)
    w_dp = din("w_dp", [D2], F32)
    w_op = din("w_op", [D, D2], F32)
    w_w1 = din("w_w1", [DF, D], F32)
    w_b1 = din("w_b1", [DF], F32)
    w_w2 = din("w_w2", [D, DF], F32)
    w_b2 = din("w_b2", [D], F32)
    w_wg = din("w_wg", [H2, D], I32)
    w_wu = din("w_wu", [H2, D], I32)
    w_wd = din("w_wd", [D, H2], I32)
    c_idf = din("c_idf", [128, 128], F32)
    c_idb = din("c_idb", [128, 128], BF16)
    c_onesb = din("c_onesb", [128, 1], BF16)
    c_ones1 = din("c_ones1", [1, 128], F32)
    c_oh = din("c_oh", [32, 32 * 128], BF16)

    out_y = nc.dram_tensor("out_y", [L, D], F32, kind="ExternalOutput").ap()
    dbg = {}
    if debug:
        for nm, shape, dt in [
                ("dbg_xfm", [128, ND * TC], F32), ("dbg_ln1b", [128, ND * TC], BF16),
                ("dbg_xp2b", [128, NDT * TC], BF16), ("dbg_siluz", [128, NDT * TC], BF16),
                ("dbg_dtb", [128, NDT * TC], BF16), ("dbg_dtinb", [RK, TC], BF16),
                ("dbg_bcs", [32, TC], BF16), ("dbg_yf", [128, NDT * TC], BF16),
                ("dbg_t1f", [128, ND * TC], F32), ("dbg_x1", [128, ND * TC], F32),
                ("dbg_ln2b", [128, ND * TC], BF16), ("dbg_hs", [128, NH * TC], BF16),
                ("dbg_state", [128, NDT * NS], F32)]:
            dbg[nm] = nc.dram_tensor(nm, shape, dt, kind="ExternalOutput").ap()

    def scratch(name, shape):
        return nc.dram_tensor(name, shape, BF16, kind="Internal").ap()

    sc_ip = scratch("sc_ip", [2 * D2, D])
    sc_xp = scratch("sc_xp", [RK + 2 * NS, D2])
    sc_dt = scratch("sc_dt", [D2, 128])
    sc_op = scratch("sc_op", [D, D2])
    sc_w1 = scratch("sc_w1", [DF, D])
    sc_w2 = scratch("sc_w2", [D, DF])
    sc_wg = scratch("sc_wg", [H2, D])
    sc_wu = scratch("sc_wu", [H2, D])
    sc_wd = scratch("sc_wd", [D, H2])
    b_scip = P.buf(); b_scxp = P.buf(); b_scdt = P.buf(); b_scop = P.buf()
    b_scw1 = P.buf(); b_scw2 = P.buf(); b_scwg = P.buf(); b_scwu = P.buf()
    b_scwd = P.buf()

    es = contextlib.ExitStack()
    with es:
        def A(name, shape, dtype=F32, psum=False):
            return Alloc(nc, P, name, shape, dtype, psum)

        s_idf = A("s_idf", [128, 128])
        s_idb = A("s_idb", [128, 128], BF16)
        s_onesb = A("s_onesb", [128, 1], BF16)
        s_ones1 = A("s_ones1", [1, 128])
        s_xpW = A("s_xpW", [128, NDT * 96], BF16)
        s_cw = A("s_cw", [128, NDT * 3])
        s_cb = A("s_cb", [128, NDT])
        s_dtbv = A("s_dtbv", [128, NDT])
        s_dp = A("s_dp", [128, NDT])
        s_b1 = A("s_b1", [128, NF])
        s_b2 = A("s_b2", [128, ND])
        s_an = A("s_an", [128, NDT * NS])
        s_state = A("s_state", [128, NDT * NS])
        state_bufs = [P.buf() for _ in range(NDT * NS)]
        s_stash = A("s_stash", [128, NDT * 2])
        stash_bufs = [P.buf() for _ in range(NDT)]
        s_eps = A("s_eps", [128, 1])
        s_one = A("s_one", [128, 1])
        scr0 = A("scr0", [128, TC])
        scr1 = A("scr1", [128, TC])
        vtmp = A("vtmp", [32, 128])
        atmp = A("atmp", [128, NDT * NS])
        aex = A("aex", [128, NDT * NS])

        p = [A(f"ps{i}", [128, 512], psum=True) for i in range(6)]
        ptr = [A(f"ptr{i}", [128, 128], psum=True) for i in range(2)]

        sems = {}
        for e in ENGS:
            sems[(e, -1)] = es.enter_context(nc.semaphore(f"sem_{e}"))
            for ln in range(DMA_LANES[e]):
                sems[(e, ln)] = es.enter_context(
                    nc.semaphore(f"sem_{e}_d{ln}"))
        sbar = es.enter_context(nc.semaphore("sem_bar"))

        left = nc._sbuf_addr_for_side("left")
        right = nc._sbuf_addr_for_side("right")
        MB = (left + 31) & ~31
        assert MB + 192 * KB <= right, (MB, right)
        mmap = []
        PRO = frozenset({-1})
        ALL = frozenset(range(9))

        def M(name, shape, dtype, off_kb, phases):
            return Manual(nc, P, mmap, name, shape, dtype,
                          MB + int(off_kb * KB), frozenset(phases))

        def dma(q, out_ap, in_ap, wb=(), rb=(), tr=False, extra=()):
            P.op(q, lambda e, o=out_ap, i=in_ap, t=tr:
                 e.dma_start(out=o, in_=i, transpose=t),
                 r=rb, w=wb, dma=True, extra=extra, dq=q + ("tr" if tr else ""))

        def mm(pa, pb, la, lb, ra, rb_, start, stop):
            P.op("PE", lambda e, o=pa, lt=la, rr=ra, s1=start, s2=stop:
                 e.matmul(o, lt, rr, start=s1, stop=s2),
                 r=(lb, rb_), w=(pb,))

        def act(out_ap, ob, in_ap, ib, func, bias=None, scale=None, rextra=()):
            kw = {}
            if bias is not None:
                kw["bias"] = bias
            if scale is not None:
                kw["scale"] = scale
            P.op("ACT", lambda e, o=out_ap, i=in_ap, f=func, k=dict(kw):
                 e.activation(out=o, in_=i, func=f, **k),
                 r=(ib,) + tuple(rextra), w=(ob,))

        def acopy(out_ap, ob, in_ap, ib):
            P.op("ACT", lambda e, o=out_ap, i=in_ap: e.copy(out=o, in_=i),
                 r=(ib,), w=(ob,))

        def vcopy(out_ap, ob, in_ap, ib):
            P.op("DVE", lambda e, o=out_ap, i=in_ap: e.tensor_copy(out=o, in_=i),
                 r=(ib,), w=(ob,))

        def tt(out_ap, ob, a_ap, ab, b_ap, bb, opk):
            P.op("DVE", lambda e, o=out_ap, i0=a_ap, i1=b_ap, k=opk:
                 e.tensor_tensor(out=o, in0=i0, in1=i1, op=k),
                 r=(ab, bb), w=(ob,))

        def ts(out_ap, ob, in_ap, ib, s1, op0, rextra=()):
            P.op("DVE", lambda e, o=out_ap, i=in_ap, s=s1, k=op0:
                 e.tensor_scalar(out=o, in0=i, scalar1=s, scalar2=None, op0=k),
                 r=(ib,) + tuple(rextra), w=(ob,))

        def stt(out_ap, ob, in0_ap, i0b, s1, in1_ap, i1b, op0, op1, rextra=()):
            P.op("DVE", lambda e, o=out_ap, i0=in0_ap, s=s1, i1=in1_ap,
                 k0=op0, k1=op1:
                 e.scalar_tensor_tensor(out=o, in0=i0, scalar=s, in1=i1,
                                        op0=k0, op1=k1),
                 r=(i0b, i1b) + tuple(rextra), w=(ob,))

        def scan(out_ap, ob, a_ap, ab, b_ap, bb, init_ap, initb):
            P.op("DVE", lambda e, o=out_ap, i0=a_ap, i1=b_ap, ii=init_ap:
                 e.tensor_tensor_scan(o, i0, i1, ii, ALU.mult, ALU.add),
                 r=(ab, bb, initb), w=(ob,), fsync=True)

        def petr(p_ap, pb, in_ap, ib, nk=128):
            P.op("PE", lambda e, o=p_ap, i=in_ap, idap=s_idf.h[0:nk, 0:nk]:
                 e.transpose(o, i, idap),
                 r=(ib, s_idf.b), w=(pb,))

        dma("SP", s_idf.h[:, :], c_idf, wb=(s_idf.b,))
        dma("SP", s_idb.h[:, :], c_idb, wb=(s_idb.b,))
        dma("SP", s_onesb.h[:, :], c_onesb, wb=(s_onesb.b,))
        dma("SP", s_ones1.h[:, :], c_ones1, wb=(s_ones1.b,))
        dma("SP", s_cw.h[:, :].rearrange("p (b k) -> p b k", k=3),
            w_cw.rearrange("(b p) k -> p b k", p=128), wb=(s_cw.b,))
        stg = M("stg", [128, 32768], BF16, 0, PRO)
        stgL = P.buf()
        stgH = P.buf()
        prep = [
            (w_ip, sc_ip, 2 * D2, D, b_scip), (w_dt, sc_dt, D2, RK, b_scdt),
            (w_op, sc_op, D, D2, b_scop), (w_w1, sc_w1, DF, D, b_scw1),
            (w_w2, sc_w2, D, DF, b_scw2), (w_wg, sc_wg, H2, D, b_scwg),
            (w_wu, sc_wu, H2, D, b_scwu), (w_wd, sc_wd, D, H2, b_scwd),
        ]
        gpar = 0

        def prep_one(src, dst, R, C, dstb, stages=None):
            nonlocal gpar
            nb = R // 128
            if stages is None:
                stages = [(stg.h[:, 0:16384], stgL),
                          (stg.h[:, 16384:32768], stgH)]
            scols = int(stages[0][0].shape[-1])
            if C > scols:
                for c0 in range(0, C, scols):
                    cc = min(scols, C - c0)
                    for b0 in range(nb):
                        sap, half = stages[gpar]
                        sv = sap[:, 0:cc].rearrange("p (b c) -> p b c", b=1)
                        dma("GP", sv,
                            src.rearrange("(b p) c -> p b c", p=128)
                            [:, b0:b0 + 1, c0:c0 + cc], wb=(half,))
                        dma("ACT",
                            dst.rearrange("(b p) c -> p b c", p=128)
                            [:, b0:b0 + 1, c0:c0 + cc], sv,
                            rb=(half,), wb=(dstb,))
                        gpar ^= 1
                return
            maxb = max(1, scols // C)
            b0 = 0
            while b0 < nb:
                g = min(maxb, nb - b0)
                sap, half = stages[gpar]
                sv = sap[:, 0:g * C].rearrange("p (b c) -> p b c", b=g)
                dma("GP", sv,
                    src.rearrange("(b p) c -> p b c", p=128)[:, b0:b0 + g, :],
                    wb=(half,))
                dma("ACT",
                    dst.rearrange("(b p) c -> p b c", p=128)[:, b0:b0 + g, 0:C],
                    sv, rb=(half,), wb=(dstb,))
                gpar ^= 1
                b0 += g

        def prep_xp():
            nonlocal gpar
            half = [stgL, stgH][gpar]
            dma("GP", stg.h[0:96, gpar * 16384: gpar * 16384 + D2], w_xp,
                wb=(half,))
            dma("ACT", sc_xp, stg.h[0:96, gpar * 16384: gpar * 16384 + D2],
                rb=(half,), wb=(b_scxp,))
            gpar ^= 1

        prep_one(*prep[0])
        prep_xp()
        prep_one(*prep[1])
        vecs = [(w_cb, D2 // 128, s_cb), (w_dtb, D2 // 128, s_dtbv),
                (w_dp, D2 // 128, s_dp), (w_b2, D // 128, s_b2),
                (w_b1, DF // 128, s_b1)]
        for (src, nblk, dst) in vecs:
            dma("SP", vtmp.h[0:nblk, :], src.rearrange("(b p) -> b p", p=128),
                wb=(vtmp.b,))
            petr(ptr[0].h[:, 0:nblk], ptr[0].b, vtmp.h[0:nblk, :], vtmp.b,
                 nk=nblk)
            vcopy(dst.h[:, :], dst.b, ptr[0].h[:, 0:nblk], ptr[0].b)
        dma("SP", atmp.h[:, :].rearrange("p (b n) -> p b n", n=NS),
            w_al.rearrange("(b p) n -> p b n", p=128), wb=(atmp.b,))
        act(aex.h[:, :], aex.b, atmp.h[:, :], atmp.b, AF.Exp)
        ts(s_an.h[:, :], s_an.b, aex.h[:, :], aex.b, -1.0, ALU.mult)
        P.op("DVE", lambda e: e.memset(s_eps.h[:, :], 1e-5), w=(s_eps.b,))
        P.op("DVE", lambda e: e.memset(s_one.h[:, :], 1.0), w=(s_one.b,))
        P.op("DVE", lambda e: e.memset(s_state.h[:, :], 0.0),
             w=tuple(state_bufs))
        P.op("DVE", lambda e: e.memset(s_stash.h[:, :], 0.0),
             w=tuple(stash_bufs))

        for tc in range(NCH):
            t0 = tc * TC
            su = f"c{tc}"
            mmap.clear()

            x_fm = M(f"x_fm{su}", [128, ND * TC], F32, 64, {0, 1, 2, 3, 4, 5})
            mbc = M(f"mbc{su}", [128, TC], F32, 188, set(range(7)))
            rbc = M(f"rbc{su}", [128, TC], F32, 190, set(range(7)))
            r_mean = M(f"rmean{su}", [1, TC], F32, 96, {0, 6})
            r_e2 = M(f"re2{su}", [1, TC], F32, 98, {0, 6})
            r_tmp = M(f"rtmp{su}", [1, TC], F32, 100, {0, 6})
            r_rstd = M(f"rrstd{su}", [1, TC], F32, 102, {0, 6})
            scr2 = M(f"scr2{su}", [128, TC], F32, 104, {1, 2})
            scr3 = M(f"scr3{su}", [128, TC], F32, 106, {1, 2})

            xstg = M(f"xstg{su}", [128, ND * TC // 2], F32, 80, {0})
            ln1b = M(f"ln1b{su}", [128, ND * TC], BF16, 128, {0, 1})
            ln1b_s = ln1b.split(ND)
            xbfA = M(f"xbfA{su}", [128, ND * TC], BF16, 136, {0})
            ipW = M(f"ipW{su}", [128, 8 * 4096], BF16, 0, {0, 1})
            ips = ipW.split(8)

            for half_i in range(2):
                for tb in range(2):
                    tb2 = half_i * 2 + tb
                    dma("SP", xstg.h[:, tb * D:(tb + 1) * D],
                        in_x[t0 + tb2 * 128: t0 + (tb2 + 1) * 128, :],
                        wb=(xstg.b,))
                for tb in range(2):
                    tb2 = half_i * 2 + tb
                    for ft in range(ND):
                        par = (tb * ND + ft) % 2
                        petr(ptr[par].h[:, :], ptr[par].b,
                             xstg.h[:, tb * D + ft * 128: tb * D + (ft + 1) * 128],
                             xstg.b)
                        vcopy(x_fm.h[:, ft * TC + tb2 * 128:
                                     ft * TC + (tb2 + 1) * 128],
                              x_fm.b, ptr[par].h[:, :], ptr[par].b)
            for kc in range(8):
                P.op("SP", lambda e, o=ipW.h[:, kc * 4096:(kc + 1) * 4096],
                     i=sc_ip[0:2 * D2, kc * 128:(kc + 1) * 128]:
                     e.dma_start(out=o, in_=i, transpose=True),
                     w=(ips[kc],), r=(b_scip,), dma=True, dq="SPtr")

            def emit_ln(srcA, outb_A, xbf_half, nb, outb_s=None):
                for hi in range(2):
                    for ft in range(nb // 2):
                        f2 = hi * (nb // 2) + ft
                        sl = srcA.h[:, f2 * TC:(f2 + 1) * TC]
                        acopy(xbf_half.h[:, ft * TC:(ft + 1) * TC], xbf_half.b,
                              sl, srcA.b)
                        act(xbf_half.h[:, (nb // 2 + ft) * TC:
                                       (nb // 2 + ft + 1) * TC],
                            xbf_half.b, sl, srcA.b, AF.Square)
                    for ft in range(nb // 2):
                        mm(p[0].h[0:1, :], p[0].b, s_onesb.h[:, :], s_onesb.b,
                           xbf_half.h[:, ft * TC:(ft + 1) * TC], xbf_half.b,
                           hi == 0 and ft == 0, hi == 1 and ft == nb // 2 - 1)
                        mm(p[1].h[0:1, :], p[1].b, s_onesb.h[:, :], s_onesb.b,
                           xbf_half.h[:, (nb // 2 + ft) * TC:
                                      (nb // 2 + ft + 1) * TC], xbf_half.b,
                           hi == 0 and ft == 0, hi == 1 and ft == nb // 2 - 1)
                ts(r_mean.h[:, :], r_mean.b, p[0].h[0:1, :], p[0].b,
                   1.0 / (nb * 128), ALU.mult)
                ts(r_e2.h[:, :], r_e2.b, p[1].h[0:1, :], p[1].b,
                   1.0 / (nb * 128), ALU.mult)
                tt(r_tmp.h[:, :], r_tmp.b, r_mean.h[:, :], r_mean.b,
                   r_mean.h[:, :], r_mean.b, ALU.mult)
                tt(r_e2.h[:, :], r_e2.b, r_e2.h[:, :], r_e2.b,
                   r_tmp.h[:, :], r_tmp.b, ALU.subtract)
                act(r_tmp.h[:, :], r_tmp.b, r_e2.h[:, :], r_e2.b, AF.Ln,
                    bias=s_eps.h[0:1, :], rextra=(s_eps.b,))
                act(r_rstd.h[:, :], r_rstd.b, r_tmp.h[:, :], r_tmp.b, AF.Exp,
                    scale=-0.5)
                mm(p[2].h[:, :], p[2].b, s_ones1.h[:, :], s_ones1.b,
                   r_mean.h[:, :], r_mean.b, True, True)
                mm(p[3].h[:, :], p[3].b, s_ones1.h[:, :], s_ones1.b,
                   r_rstd.h[:, :], r_rstd.b, True, True)
                acopy(mbc.h[:, :], mbc.b, p[2].h[:, :], p[2].b)
                acopy(rbc.h[:, :], rbc.b, p[3].h[:, :], p[3].b)
                for ft in range(nb):
                    sl = srcA.h[:, ft * TC:(ft + 1) * TC]
                    ob = outb_s[ft] if outb_s is not None else outb_A.b
                    tt(scr0.h[:, :], scr0.b, sl, srcA.b, mbc.h[:, :], mbc.b,
                       ALU.subtract)
                    tt(outb_A.h[:, ft * TC:(ft + 1) * TC], ob,
                       scr0.h[:, :], scr0.b, rbc.h[:, :], rbc.b, ALU.mult)

            emit_ln(x_fm, ln1b, xbfA, ND, outb_s=ln1b_s)
            if debug and tc == 0:
                dma("GP", dbg["dbg_xfm"], x_fm.h[:, :], rb=(x_fm.b,))
                dma("GP", dbg["dbg_ln1b"], ln1b.h[:, :], rb=(ln1b.b,))

            siluzb = M(f"siluzb{su}", [128, NDT * TC], BF16, 112, {1, 2, 3})
            xp2b = M(f"xp2b{su}", [128, NDT * TC], BF16, 136, {1, 2, 3})
            xp2b_s = xp2b.split(NDT)
            xpt = [M(f"xpt{i}{su}", [128, 516], F32, 152 + i * 2.0625, {1})
                   for i in range(2)]
            for ft in range(32):
                pb = p[ft % 2]
                for kc in range(8):
                    mm(pb.h[:, :], pb.b,
                       ipW.h[:, kc * 4096 + ft * 128: kc * 4096 + (ft + 1) * 128],
                       ips[kc], ln1b.h[:, kc * TC:(kc + 1) * TC], ln1b_s[kc],
                       kc == 0, kc == 7)
                if ft < NDT:
                    dtile = ft
                    par = dtile % 2
                    xt = xpt[par]
                    sA = [scr0, scr2][par]
                    sB = [scr1, scr3][par]
                    vcopy(xt.h[:, 0:2], xt.b,
                          s_stash.h[:, dtile * 2: dtile * 2 + 2],
                          stash_bufs[dtile])
                    acopy(xt.h[:, 2:514], xt.b, pb.h[:, :], pb.b)
                    vcopy(s_stash.h[:, dtile * 2: dtile * 2 + 2],
                          stash_bufs[dtile], xt.h[:, 512:514], xt.b)
                    k0 = s_cw.h[:, dtile * 3 + 0: dtile * 3 + 1]
                    k1 = s_cw.h[:, dtile * 3 + 1: dtile * 3 + 2]
                    k2 = s_cw.h[:, dtile * 3 + 2: dtile * 3 + 3]
                    ts(sA.h[:, :], sA.b, xt.h[:, 0:512], xt.b, k0, ALU.mult,
                       rextra=(s_cw.b,))
                    stt(sB.h[:, :], sB.b, xt.h[:, 1:513], xt.b, k1,
                        sA.h[:, :], sA.b, ALU.mult, ALU.add, rextra=(s_cw.b,))
                    stt(sA.h[:, :], sA.b, xt.h[:, 2:514], xt.b, k2,
                        sB.h[:, :], sB.b, ALU.mult, ALU.add, rextra=(s_cw.b,))
                    act(xp2b.h[:, dtile * TC:(dtile + 1) * TC], xp2b_s[dtile],
                        sA.h[:, :], sA.b, AF.Silu,
                        bias=s_cb.h[:, dtile:dtile + 1], rextra=(s_cb.b,))
                else:
                    zt = ft - NDT
                    act(siluzb.h[:, zt * TC:(zt + 1) * TC], siluzb.b,
                        pb.h[:, :], pb.b, AF.Silu)
            if debug and tc == 0:
                dma("GP", dbg["dbg_xp2b"], xp2b.h[:, :], rb=(xp2b.b,))
                dma("GP", dbg["dbg_siluz"], siluzb.h[:, :], rb=(siluzb.b,))

            dtb16 = M(f"dtb16{su}", [128, NDT * TC], BF16, 32, {2, 3})
            dtb16_s = dtb16.split(NDT)
            dtinb = M(f"dtinb{su}", [RK, TC], BF16, 98, {2})
            bcs = M(f"bcs{su}", [32, TC], BF16, 99, {2, 3})
            dtWc = M(f"dtWc{su}", [128, D2], BF16, 88, {2})
            if tc == 0:
                for kc in range(NDT):
                    dma("SP", s_xpW.h[:, kc * 96:(kc + 1) * 96],
                        sc_xp[0:96, kc * 128:(kc + 1) * 128], wb=(s_xpW.b,),
                        rb=(b_scxp,), tr=True)
            dma("SP", dtWc.h[:, :], sc_dt[0:D2, 0:128], wb=(dtWc.b,), rb=(b_scdt,), tr=True)
            if tc == 0:
                dstg = [M(f"dstg{i}", [128, 2048], BF16, 176 + i * 4,
                          {2, 3, 3.5}) for i in range(2)]
                dstages = [(dstg[0].h[:, :], dstg[0].b),
                           (dstg[1].h[:, :], dstg[1].b)]

                def _prep_pairs():
                    gp = [0]
                    scols = 2048
                    for (srcw, dstw, R, C, dstb) in prep[2:]:
                        nb = R // 128
                        for c0 in range(0, C, scols):
                            cc = min(scols, C - c0)
                            gmax = max(1, scols // C) if C <= scols else 1
                            b0 = 0
                            while b0 < nb:
                                g = min(gmax, nb - b0)
                                sap, half = dstages[gp[0]]

                                def emit(srcw=srcw, dstw=dstw, b0=b0, g=g,
                                         c0=c0, cc=cc, sap=sap, half=half,
                                         dstb=dstb):
                                    sv = sap[:, 0:g * cc].rearrange(
                                        "p (b c) -> p b c", b=g)
                                    dma("GP", sv, srcw.rearrange(
                                        "(b p) c -> p b c", p=128)
                                        [:, b0:b0 + g, c0:c0 + cc], wb=(half,))
                                    dma("ACT", dstw.rearrange(
                                        "(b p) c -> p b c", p=128)
                                        [:, b0:b0 + g, c0:c0 + cc], sv,
                                        rb=(half,), wb=(dstb,))
                                yield emit
                                gp[0] ^= 1
                                b0 += g
                            if C <= scols:
                                break

                prep_iter = _prep_pairs()
            else:
                prep_iter = iter(())
            for kc in range(NDT):
                mm(p[5].h[0:96, :], p[5].b, s_xpW.h[:, kc * 96:(kc + 1) * 96],
                   s_xpW.b, xp2b.h[:, kc * TC:(kc + 1) * TC], xp2b_s[kc],
                   kc == 0, kc == NDT - 1)
            acopy(dtinb.h[:, :], dtinb.b, p[5].h[0:RK, :], p[5].b)
            acopy(bcs.h[:, :], bcs.b, p[5].h[RK:96, :], p[5].b)
            for dtile in range(NDT):
                pb = p[dtile % 2]
                mm(pb.h[:, :], pb.b,
                   dtWc.h[0:RK, dtile * 128:(dtile + 1) * 128], dtWc.b,
                   dtinb.h[:, :], dtinb.b, True, True)
                par = dtile % 2
                sA = [scr0, scr2][par]
                sB = [scr1, scr3][par]
                ts(sA.h[:, :], sA.b, pb.h[:, :], pb.b,
                   s_dtbv.h[:, dtile:dtile + 1], ALU.add, rextra=(s_dtbv.b,))
                act(sB.h[:, :], sB.b, sA.h[:, :], sA.b, AF.Abs)
                act(sB.h[:, :], sB.b, sB.h[:, :], sB.b, AF.Exp, scale=-1.0)
                act(sB.h[:, :], sB.b, sB.h[:, :], sB.b, AF.Ln,
                    bias=s_one.h[:, :], rextra=(s_one.b,))
                stt(dtb16.h[:, dtile * TC:(dtile + 1) * TC], dtb16_s[dtile],
                    sA.h[:, :], sA.b, 0.0, sB.h[:, :], sB.b, ALU.max, ALU.add)
            if debug and tc == 0:
                dma("GP", dbg["dbg_dtb"], dtb16.h[:, :], rb=(dtb16.b,))
                dma("GP", dbg["dbg_dtinb"], dtinb.h[:, :], rb=(dtinb.b,))
                dma("GP", dbg["dbg_bcs"], bcs.h[:, :], rb=(bcs.b,))

            Bbc = M(f"Bbc{su}", [128, NS * TC], BF16, 0, {3})
            Bbc_s = Bbc.split(NS)
            Cbc = M(f"Cbc{su}", [128, NS * TC], BF16, 16, {3})
            Cbc_s = Cbc.split(NS)
            yf = M(f"yf{su}", [128, NDT * TC], BF16, 48, {3, 4})
            s_ohD = M(f"soh{su}", [32, 32 * 128], BF16, 80, {3})
            DEPTH = 4
            a_t = [M(f"a{i}{su}", [128, TC], F32, 80 + i * 2, {3.5})
                   for i in range(DEPTH)]
            b_t = [M(f"b{i}{su}", [128, TC], BF16, 88 + i * 1, {3.5})
                   for i in range(DEPTH)]
            h_t = [M(f"h{i}{su}", [128, TC], F32, 168 + i * 2, {3.5})
                   for i in range(DEPTH)]
            ch_t = [M(f"ch{i}{su}", [128, TC], BF16, 184 + i * 1, {3.5})
                    for i in range(DEPTH)]
            hb_t = [M(f"hb{i}{su}", [128, TC], BF16, 108 + i * 1, {3.5})
                    for i in range(DEPTH)]
            u_t = [M(f"u{i}{su}", [128, TC], BF16, [96, 102][i], {3.5})
                     for i in range(2)]
            xpd_t = [M(f"xpd{i}{su}", [128, TC], BF16, 100 + i, {3.5})
                     for i in range(2)]
            dma("SP", s_ohD.h[:, :], c_oh, wb=(s_ohD.b,))
            for n in range(NS):
                mm(p[2].h[:, :], p[2].b, s_ohD.h[:, n * 128:(n + 1) * 128],
                   s_ohD.b, bcs.h[:, :], bcs.b, True, True)
                acopy(Bbc.h[:, n * TC:(n + 1) * TC], Bbc_s[n], p[2].h[:, :],
                      p[2].b)
                mm(p[3].h[:, :], p[3].b,
                   s_ohD.h[:, (NS + n) * 128:(NS + n + 1) * 128], s_ohD.b,
                   bcs.h[:, :], bcs.b, True, True)
                acopy(Cbc.h[:, n * TC:(n + 1) * TC], Cbc_s[n], p[3].h[:, :],
                      p[3].b)
            pend_save = [None]
            pend_ch = []

            def emit_save():
                if pend_save[0] is not None:
                    (sidx, spar) = pend_save[0]
                    acopy(s_state.h[:, sidx:sidx + 1], state_bufs[sidx],
                          h_t[spar].h[:, TC - 1:TC], h_t[spar].b)
                    acopy(hb_t[spar].h[:, :], hb_t[spar].b,
                          h_t[spar].h[:, :], h_t[spar].b)
                    pend_save[0] = None

            def emit_ch(upto):
                while pend_ch and pend_ch[0][0] <= upto:
                    (cidx, cpar, cn, cpyp) = pend_ch.pop(0)
                    tt(ch_t[cpar].h[:, :], ch_t[cpar].b, hb_t[cpar].h[:, :],
                       hb_t[cpar].b, Cbc.h[:, cn * TC:(cn + 1) * TC], Cbc_s[cn],
                       ALU.mult)
                    mm(cpyp.h[:, :], cpyp.b, s_idb.h[:, :], s_idb.b,
                       ch_t[cpar].h[:, :], ch_t[cpar].b, False, cn == NS - 1)

            for dtile in range(NDT):
                dsl = slice(dtile * TC, (dtile + 1) * TC)
                dpar = dtile % 2
                pyp = p[4 + dpar]
                tt(u_t[dpar].h[:, :], u_t[dpar].b, dtb16.h[:, dsl],
                   dtb16_s[dtile], xp2b.h[:, dsl], xp2b_s[dtile], ALU.mult)
                act(xpd_t[dpar].h[:, :], xpd_t[dpar].b, xp2b.h[:, dsl],
                    xp2b_s[dtile], AF.Copy, scale=s_dp.h[:, dtile:dtile + 1],
                    rextra=(s_dp.b,))
                mm(pyp.h[:, :], pyp.b, s_idb.h[:, :], s_idb.b,
                   xpd_t[dpar].h[:, :], xpd_t[dpar].b, True, False)
                for n in range(NS):
                    idx = dtile * NS + n
                    par = idx % DEPTH
                    act(a_t[par].h[:, :], a_t[par].b, dtb16.h[:, dsl],
                        dtb16_s[dtile], AF.Exp, scale=s_an.h[:, idx:idx + 1],
                        rextra=(s_an.b,))
                    tt(b_t[par].h[:, :], b_t[par].b, u_t[dpar].h[:, :],
                       u_t[dpar].b, Bbc.h[:, n * TC:(n + 1) * TC], Bbc_s[n],
                       ALU.mult)
                    scan(h_t[par].h[:, :], h_t[par].b, a_t[par].h[:, :],
                         a_t[par].b, b_t[par].h[:, :], b_t[par].b,
                         s_state.h[:, idx:idx + 1], state_bufs[idx])
                    emit_save()
                    pend_save[0] = (idx, par)
                    nxt = next(prep_iter, None)
                    if nxt is not None:
                        nxt()
                    pend_ch.append((idx, par, n, pyp))
                    emit_ch(idx - 2)
                emit_ch(dtile * NS + NS - 1)
                tt(yf.h[:, dsl], yf.b, pyp.h[:, :], pyp.b,
                   siluzb.h[:, dsl], siluzb.b, ALU.mult)
            emit_save()
            if debug and tc == 0:
                dma("GP", dbg["dbg_yf"], yf.h[:, :], rb=(yf.b,))
                dma("GP", dbg["dbg_state"], s_state.h[:, :], rb=(s_state.b,))

            opW = M(f"opW{su}", [128, NDT * 1024], BF16, 0, {4})
            ops_ = opW.split(NDT)
            t1f = M(f"t1f{su}", [128, ND * TC], F32, 96, {4, 5})
            t1b = M(f"t1b{su}", [128, ND * TC], BF16, 128, {4, 5})
            for kc in range(NDT):
                P.op("SP", lambda e, o=opW.h[:, kc * 1024:(kc + 1) * 1024],
                     i=sc_op[0:D, kc * 128:(kc + 1) * 128]:
                     e.dma_start(out=o, in_=i, transpose=True),
                     w=(ops_[kc],), r=(b_scop,), dma=True, dq="SPtr")
            for ft in range(ND):
                pb = p[ft % 2]
                for kc in range(NDT):
                    mm(pb.h[:, :], pb.b,
                       opW.h[:, kc * 1024 + ft * 128: kc * 1024 + (ft + 1) * 128],
                       ops_[kc], yf.h[:, kc * TC:(kc + 1) * TC], yf.b,
                       kc == 0, kc == NDT - 1)
                par = ft % 2
                fsl = slice(ft * TC, (ft + 1) * TC)
                sA = [scr0, scr1][par]
                tt(sA.h[:, :], sA.b, x_fm.h[:, fsl], x_fm.b,
                   mbc.h[:, :], mbc.b, ALU.subtract)
                tt(sA.h[:, :], sA.b, sA.h[:, :], sA.b, rbc.h[:, :], rbc.b,
                   ALU.mult)
                tt(t1f.h[:, fsl], t1f.b, pb.h[:, :], pb.b, sA.h[:, :], sA.b,
                   ALU.add)
                acopy(t1b.h[:, fsl], t1b.b, t1f.h[:, fsl], t1f.b)
            if debug and tc == 0:
                dma("GP", dbg["dbg_t1f"], t1f.h[:, :], rb=(t1f.b,))

            w1W = M(f"w1W{su}", [128, 8 * 4096], BF16, 0, {5})
            w1s = w1W.split(8)
            hgelA = M(f"hgelA{su}", [128, NF * TC // 2], BF16, 112, {5})
            hgelB = M(f"hgelB{su}", [128, NF * TC // 2], BF16, 80, {5})
            x1 = M(f"x1{su}", [128, ND * TC], F32, 152, {5, 6, 7, 8})
            for kc in range(8):
                P.op("SP", lambda e, o=w1W.h[:, kc * 4096:(kc + 1) * 4096],
                     i=sc_w1[0:DF, kc * 128:(kc + 1) * 128]:
                     e.dma_start(out=o, in_=i, transpose=True),
                     w=(w1s[kc],), r=(b_scw1,), dma=True, dq="SPtr")

            def hgel_ap(kb):
                if kb < 16:
                    return hgelA.h[:, kb * TC:(kb + 1) * TC], hgelA.b
                return hgelB.h[:, (kb - 16) * TC:(kb - 15) * TC], hgelB.b

            for ft in range(NF):
                pb = p[ft % 2]
                for kc in range(8):
                    mm(pb.h[:, :], pb.b,
                       w1W.h[:, kc * 4096 + ft * 128: kc * 4096 + (ft + 1) * 128],
                       w1s[kc], t1b.h[:, kc * TC:(kc + 1) * TC], t1b.b,
                       kc == 0, kc == 7)
                ga, gb = hgel_ap(ft)
                act(ga, gb, pb.h[:, :], pb.b, AF.Gelu,
                    bias=s_b1.h[:, ft:ft + 1], rextra=(s_b1.b,))
            for kb in range(32):
                P.op("SP", lambda e, o=w1W.h[:, kb * 1024:(kb + 1) * 1024],
                     i=sc_w2[0:D, kb * 128:(kb + 1) * 128]:
                     e.dma_start(out=o, in_=i, transpose=True),
                     w=(w1s[kb // 4],), r=(b_scw2,), dma=True, dq="SPtr")
            for ft in range(ND):
                pb = p[2 + ft % 2]
                for kb in range(32):
                    ga, gb = hgel_ap(kb)
                    mm(pb.h[:, :], pb.b,
                       w1W.h[:, kb * 1024 + ft * 128: kb * 1024 + (ft + 1) * 128],
                       w1s[kb // 4], ga, gb, kb == 0, kb == 31)
                par = ft % 2
                fsl = slice(ft * TC, (ft + 1) * TC)
                sA = [scr0, scr1][par]
                stt(sA.h[:, :], sA.b, pb.h[:, :], pb.b,
                    s_b2.h[:, ft:ft + 1], t1f.h[:, fsl], t1f.b,
                    ALU.add, ALU.add, rextra=(s_b2.b,))
                tt(x1.h[:, fsl], x1.b, sA.h[:, :], sA.b, x_fm.h[:, fsl], x_fm.b,
                   ALU.add)
            if debug and tc == 0:
                dma("GP", dbg["dbg_x1"], x1.h[:, :], rb=(x1.b,))

            ln2b = M(f"ln2b{su}", [128, ND * TC], BF16, 128, {6, 7})
            xbfG = M(f"xbfG{su}", [128, ND * TC], BF16, 136, {6})
            emit_ln(x1, ln2b, xbfG, ND)
            if debug and tc == 0:
                dma("GP", dbg["dbg_ln2b"], ln2b.h[:, :], rb=(ln2b.b,))

            wgW = M(f"wgW{su}", [128, 8 * H2], BF16, 0, {7})
            wgs = wgW.split(8)
            wuW = M(f"wuW{su}", [128, 8 * H2], BF16, 48, {7})
            wus = wuW.split(8)
            hs = M(f"hs{su}", [128, NH * TC], BF16, 96, {7, 8})
            for kc in range(8):
                P.op("SP", lambda e, o=wgW.h[:, kc * H2:(kc + 1) * H2],
                     i=sc_wg[0:H2, kc * 128:(kc + 1) * 128]:
                     e.dma_start(out=o, in_=i, transpose=True),
                     w=(wgs[kc],), r=(b_scwg,), dma=True, dq="SPtr")
                P.op("SP", lambda e, o=wuW.h[:, kc * H2:(kc + 1) * H2],
                     i=sc_wu[0:H2, kc * 128:(kc + 1) * 128]:
                     e.dma_start(out=o, in_=i, transpose=True),
                     w=(wus[kc],), r=(b_scwu,), dma=True, dq="SPtr")
            for ft in range(NH):
                par = ft % 2
                pg = p[par]
                pu = p[2 + par]
                for kc in range(8):
                    mm(pg.h[:, :], pg.b,
                       wgW.h[:, kc * H2 + ft * 128: kc * H2 + (ft + 1) * 128],
                       wgs[kc], ln2b.h[:, kc * TC:(kc + 1) * TC], ln2b.b,
                       kc == 0, kc == 7)
                for kc in range(8):
                    mm(pu.h[:, :], pu.b,
                       wuW.h[:, kc * H2 + ft * 128: kc * H2 + (ft + 1) * 128],
                       wus[kc], ln2b.h[:, kc * TC:(kc + 1) * TC], ln2b.b,
                       kc == 0, kc == 7)
                sA = [scr0, scr1][par]
                act(sA.h[:, :], sA.b, pg.h[:, :], pg.b, AF.Silu, scale=QSCALE)
                tt(hs.h[:, ft * TC:(ft + 1) * TC], hs.b, sA.h[:, :], sA.b,
                   pu.h[:, :], pu.b, ALU.mult)
            if debug and tc == 0:
                dma("GP", dbg["dbg_hs"], hs.h[:, :], rb=(hs.b,))

            wdW = M(f"wdW{su}", [128, NH * 1024], BF16, 0, {8})
            wds = wdW.split(NH)
            outF = M(f"outF{su}", [128, ND * TC], F32, 64, {8})
            outtm = M(f"outtm{su}", [128, 4 * D], F32, 80, {8})
            for kb in range(NH):
                P.op("SP", lambda e, o=wdW.h[:, kb * 1024:(kb + 1) * 1024],
                     i=sc_wd[0:D, kb * 128:(kb + 1) * 128]:
                     e.dma_start(out=o, in_=i, transpose=True),
                     w=(wds[kb],), r=(b_scwd,), dma=True, dq="SPtr")
            for ft in range(ND):
                pb = p[ft % 2]
                for kb in range(NH):
                    mm(pb.h[:, :], pb.b,
                       wdW.h[:, kb * 1024 + ft * 128: kb * 1024 + (ft + 1) * 128],
                       wds[kb], hs.h[:, kb * TC:(kb + 1) * TC], hs.b,
                       kb == 0, kb == NH - 1)
                fsl = slice(ft * TC, (ft + 1) * TC)
                stt(outF.h[:, fsl], outF.b, pb.h[:, :], pb.b,
                    QSCALE * QSCALE, x1.h[:, fsl], x1.b, ALU.mult, ALU.add)
            for tb in range(4):
                for ft in range(ND):
                    par = (tb * ND + ft) % 2
                    petr(ptr[par].h[:, :], ptr[par].b,
                         outF.h[:, ft * TC + tb * 128: ft * TC + (tb + 1) * 128],
                         outF.b)
                    vcopy(outtm.h[:, tb * D + ft * 128: tb * D + (ft + 1) * 128],
                          outtm.b, ptr[par].h[:, :], ptr[par].b)
            for tb in range(4):
                dma("GP", out_y[t0 + tb * 128: t0 + (tb + 1) * 128, :],
                    outtm.h[:, tb * D:(tb + 1) * D], rb=(outtm.b,))

        P.barrier()
        P.assign_values()
        with nc.Block() as blk:
            blk.sync(lambda e: P.emit_engine("SP", e, sems, sbar))
            blk.tensor(lambda e: P.emit_engine("PE", e, sems, sbar))
            blk.scalar(lambda e: P.emit_engine("ACT", e, sems, sbar))
            blk.vector(lambda e: P.emit_engine("DVE", e, sems, sbar))
            blk.gpsimd(lambda e: P.emit_engine("GP", e, sems, sbar))

    return nc


def _consts():
    oh = np.zeros((32, 32 * 128), np.float32)
    for j in range(32):
        oh[j, j * 128:(j + 1) * 128] = 1.0
    return {
        "c_idf": np.eye(128, dtype=np.float32),
        "c_idb": np.eye(128, dtype=np.float32).astype(ml_dtypes.bfloat16),
        "c_onesb": np.ones((128, 1), np.float32).astype(ml_dtypes.bfloat16),
        "c_ones1": np.ones((1, 128), np.float32),
        "c_oh": oh.astype(ml_dtypes.bfloat16),
    }


_NAMES = [
    ("w_ip", "L0_attn_in_proj_w", np.float32),
    ("w_cw", "L0_attn_conv_w", np.float32),
    ("w_cb", "L0_attn_conv_b", np.float32),
    ("w_xp", "L0_attn_x_proj_w", np.float32),
    ("w_dt", "L0_attn_dt_proj_w", np.float32),
    ("w_dtb", "L0_attn_dt_proj_b", np.float32),
    ("w_al", "L0_attn_A_log", np.float32),
    ("w_dp", "L0_attn_Dp", np.float32),
    ("w_op", "L0_attn_out_proj_w", np.float32),
    ("w_w1", "L0_attn_w1", np.float32),
    ("w_b1", "L0_attn_b1", np.float32),
    ("w_w2", "L0_attn_w2", np.float32),
    ("w_b2", "L0_attn_b2", np.float32),
    ("w_wg", "L0_ffn_w_gate", np.int32),
    ("w_wu", "L0_ffn_w_up", np.int32),
    ("w_wd", "L0_ffn_w_down", np.int32),
]


def kernel(**inputs):
    nc = build_nc()
    base = dict(_consts())
    for (k, src, dt) in _NAMES:
        base[k] = np.ascontiguousarray(np.asarray(inputs[src], dt))
    x = np.ascontiguousarray(np.asarray(inputs["L0_attn_x"], np.float32))
    in_maps = [dict(base, in_x=np.ascontiguousarray(x[b])) for b in range(8)]
    res = run_bass_kernel_spmd(nc, in_maps, core_ids=list(range(8)))
    return np.stack(
        [np.asarray(res.results[b]["out_y"], np.float32) for b in range(8)],
        axis=0)


# version 34
# speedup vs baseline: 1.0292x; 1.0292x over previous
import contextlib
import numpy as np
import ml_dtypes
import concourse.bass as bass
import concourse.mybir as mybir
from concourse.bass_utils import run_bass_kernel_spmd

F32 = mybir.dt.float32
BF16 = mybir.dt.bfloat16
I32 = mybir.dt.int32
AF = mybir.ActivationFunctionType
ALU = mybir.AluOpType

D = 1024
D2 = 2048
L = 2048
TC = 512
NCH = L // TC
NDT = D2 // 128
ND = D // 128
NS = 16
RK = 64
DF = 4096
NF = DF // 128
H2 = 3072
NH = H2 // 128
QSCALE = 1.0 / 127.0
KB = 1024

ENGS = ("SP", "PE", "ACT", "DVE", "GP")
USE_DWAITS = False


class _Op:
    __slots__ = ("eng", "emit", "dma", "inc", "val", "deps", "seq", "bar",
                 "own_wait", "bar_target", "snapshot", "fsync", "dq", "skey",
                 "dwaits")

    def __init__(self):
        self.bar = False
        self.inc = False
        self.val = None
        self.deps = set()
        self.fsync = False
        self.dq = None


class _Buf:
    __slots__ = ("writes", "readers", "alias_prev")

    def __init__(self):
        self.writes = []
        self.readers = []
        self.alias_prev = []


DMA_LANES = {"SP": 4, "GP": 2, "ACT": 2, "DVE": 1, "PE": 1}


class Prog:
    def __init__(self):
        self.ops = []
        self.bufs = []
        self.nbar = 0
        self.last_per_engine = {e: None for e in ENGS}
        self.dma_count = {e: 0 for e in ENGS}
        self.lane_counts = {}

    def buf(self):
        b = _Buf()
        self.bufs.append(b)
        return b

    def _needs(self, d, o):
        if d is o:
            return False
        if d.fsync:
            return True
        if d.eng == o.eng:
            if d.dma and o.dma:
                return d.dq != o.dq
            if (not d.dma) and (not o.dma):
                return False
            return True
        return True

    def op(self, eng, emit, r=(), w=(), dma=False, extra=(), fsync=False,
           dq=None):
        o = _Op()
        o.eng = eng
        o.emit = emit
        o.dma = dma
        o.inc = dma
        o.fsync = fsync
        o.dq = dq if dq is not None else (eng if dma else None)
        o.dwaits = {}
        for d in o.deps:
            if d.dma:
                o.dwaits[d.skey] = self.lane_counts[d.skey]
        if dma:
            lane = self.dma_count[eng] % DMA_LANES[eng]
            self.dma_count[eng] += 1
            o.skey = (eng, lane)
            self.lane_counts[o.skey] = self.lane_counts.get(o.skey, 0) + 16
        else:
            o.skey = (eng, -1)
        for b in r:
            for d in b.writes:
                if self._needs(d, o):
                    o.deps.add(d)
        for b in w:
            for d in b.writes:
                if self._needs(d, o):
                    o.deps.add(d)
            for d in b.readers:
                if self._needs(d, o):
                    o.deps.add(d)
            for ob in b.alias_prev:
                for d in ob.writes + ob.readers:
                    if self._needs(d, o):
                        o.deps.add(d)
        for d in extra:
            if self._needs(d, o):
                o.deps.add(d)
        for d in o.deps:
            d.inc = True
        for b in r:
            b.readers.append(o)
        for b in w:
            b.writes = [o]
            b.readers = []
            b.alias_prev = []
        o.seq = len(self.ops)
        self.ops.append(o)
        self.last_per_engine[eng] = o
        return o

    def barrier(self):
        self.nbar += 1
        for e in ENGS:
            o = _Op()
            o.eng = e
            o.emit = None
            o.dma = False
            o.bar = True
            last = self.last_per_engine[e]
            if last is not None and not last.bar:
                last.inc = True
            o.own_wait = None
            o.bar_target = 5 * self.nbar
            o.seq = len(self.ops)
            self.ops.append(o)
            self.last_per_engine[e] = o
        for b in self.bufs:
            b.writes = []
            b.readers = []

    def assign_values(self):
        cnt = {}
        for o in self.ops:
            if o.bar:
                o.snapshot = dict(cnt)
            elif o.inc:
                cnt[o.skey] = cnt.get(o.skey, 0) + (16 if o.dma else 1)
                o.val = cnt[o.skey]

    def emit_engine(self, eng, engobj, sems, sbar):
        waited = {}
        for o in self.ops:
            if o.eng != eng:
                continue
            if o.bar:
                for k, v in o.snapshot.items():
                    if k[0] == eng and waited.get(k, 0) < v:
                        engobj.wait_ge(sems[k], v)
                        waited[k] = v
                engobj.sem_inc(sbar, 1)
                engobj.wait_ge(sbar, o.bar_target)
                for k, v in o.snapshot.items():
                    waited[k] = max(waited.get(k, 0), v)
                continue
            merged = dict(o.dwaits) if USE_DWAITS else {}
            for dp in o.deps:
                if USE_DWAITS and dp.dma:
                    continue
                assert dp.val is not None
                if merged.get(dp.skey, 0) < dp.val:
                    merged[dp.skey] = dp.val
            for s, v in merged.items():
                if waited.get(s, 0) < v:
                    engobj.wait_ge(sems[s], v)
                    waited[s] = v
            inst = o.emit(engobj)
            if o.inc:
                inst.then_inc(sems[o.skey], 16 if o.dma else 1)


class Alloc:

    def __init__(self, nc, P, name, shape, dtype, psum=False):
        self.cm = (nc.psum_tensor if psum else nc.sbuf_tensor)(name, shape, dtype)
        self.h = self.cm.__enter__()
        self.b = P.buf()


_DT_BYTES = {F32: 4, BF16: 2, I32: 4}


class Manual:

    seq = [0]
    registry = []

    def __init__(self, nc, P, mmap, name, shape, dtype, off, phases):
        nbytes = int(np.prod(shape[1:])) * _DT_BYTES[dtype]
        for (o2, n2, ph2, nm2) in mmap:
            if phases & ph2 and off < o2 + n2 and o2 < off + nbytes:
                raise AssertionError(
                    f"sbuf overlap {name}@[{off},{off+nbytes}) ph{phases} vs "
                    f"{nm2}@[{o2},{o2+n2}) ph{ph2}")
        mmap.append((off, nbytes, set(phases), name))
        Manual.seq[0] += 1
        self.h = nc.alloc_sbuf_tensor_at(
            f"{name}_{Manual.seq[0]}", shape, dtype, offset=off)
        self.b = P.buf()
        lo, hi = off, off + nbytes
        keep = []
        for (l2, h2, b2) in Manual.registry:
            if l2 < hi and lo < h2:
                self.b.alias_prev.append(b2)
                if l2 >= lo and h2 <= hi:
                    continue
            keep.append((l2, h2, b2))
        keep.append((lo, hi, self.b))
        Manual.registry = keep
        self._lo, self._hi, self._P = lo, hi, P

    def split(self, n):
        bufs = [self._P.buf() for _ in range(n)]
        sz = (self._hi - self._lo) // n
        Manual.registry = [e for e in Manual.registry if e[2] is not self.b]
        for i, b in enumerate(bufs):
            b.alias_prev = list(self.b.alias_prev)
            Manual.registry.append(
                (self._lo + i * sz, self._lo + (i + 1) * sz, b))
        return bufs


def build_nc(debug=False):
    nc = bass.Bass("TRN2", target_bir_lowering=False, debug=False)
    P = Prog()
    Manual.registry = []

    d_in = {}

    def din(name, shape, dt):
        d_in[name] = nc.dram_tensor(name, shape, dt, kind="ExternalInput").ap()
        return d_in[name]

    in_x = din("in_x", [L, D], F32)
    w_ip = din("w_ip", [2 * D2, D], F32)
    w_cw = din("w_cw", [D2, 3], F32)
    w_cb = din("w_cb", [D2], F32)
    w_xp = din("w_xp", [RK + 2 * NS, D2], F32)
    w_dt = din("w_dt", [D2, RK], F32)
    w_dtb = din("w_dtb", [D2], F32)
    w_al = din("w_al", [D2, NS], F32)
    w_dp = din("w_dp", [D2], F32)
    w_op = din("w_op", [D, D2], F32)
    w_w1 = din("w_w1", [DF, D], F32)
    w_b1 = din("w_b1", [DF], F32)
    w_w2 = din("w_w2", [D, DF], F32)
    w_b2 = din("w_b2", [D], F32)
    w_wg = din("w_wg", [H2, D], I32)
    w_wu = din("w_wu", [H2, D], I32)
    w_wd = din("w_wd", [D, H2], I32)
    c_idf = din("c_idf", [128, 128], F32)
    c_idb = din("c_idb", [128, 128], BF16)
    c_onesb = din("c_onesb", [128, 1], BF16)
    c_ones1 = din("c_ones1", [1, 128], F32)
    c_oh = din("c_oh", [32, 32 * 128], BF16)

    out_y = nc.dram_tensor("out_y", [L, D], F32, kind="ExternalOutput").ap()
    dbg = {}
    if debug:
        for nm, shape, dt in [
                ("dbg_xfm", [128, ND * TC], F32), ("dbg_ln1b", [128, ND * TC], BF16),
                ("dbg_xp2b", [128, NDT * TC], BF16), ("dbg_siluz", [128, NDT * TC], BF16),
                ("dbg_dtb", [128, NDT * TC], BF16), ("dbg_dtinb", [RK, TC], BF16),
                ("dbg_bcs", [32, TC], BF16), ("dbg_yf", [128, NDT * TC], BF16),
                ("dbg_t1f", [128, ND * TC], F32), ("dbg_x1", [128, ND * TC], F32),
                ("dbg_ln2b", [128, ND * TC], BF16), ("dbg_hs", [128, NH * TC], BF16),
                ("dbg_state", [128, NDT * NS], F32)]:
            dbg[nm] = nc.dram_tensor(nm, shape, dt, kind="ExternalOutput").ap()

    def scratch(name, shape):
        return nc.dram_tensor(name, shape, BF16, kind="Internal").ap()

    sc_ip = scratch("sc_ip", [2 * D2, D])
    sc_xp = scratch("sc_xp", [RK + 2 * NS, D2])
    sc_dt = scratch("sc_dt", [D2, 128])
    sc_op = scratch("sc_op", [D, D2])
    sc_w1 = scratch("sc_w1", [DF, D])
    sc_w2 = scratch("sc_w2", [D, DF])
    sc_wg = scratch("sc_wg", [H2, D])
    sc_wu = scratch("sc_wu", [H2, D])
    sc_wd = scratch("sc_wd", [D, H2])
    b_scip = P.buf(); b_scxp = P.buf(); b_scdt = P.buf(); b_scop = P.buf()
    b_scw1 = P.buf(); b_scw2 = P.buf(); b_scwg = P.buf(); b_scwu = P.buf()
    b_scwd = P.buf()

    es = contextlib.ExitStack()
    with es:
        def A(name, shape, dtype=F32, psum=False):
            return Alloc(nc, P, name, shape, dtype, psum)

        s_idf = A("s_idf", [128, 128])
        s_idb = A("s_idb", [128, 128], BF16)
        s_onesb = A("s_onesb", [128, 1], BF16)
        s_ones1 = A("s_ones1", [1, 128])
        s_xpW = A("s_xpW", [128, NDT * 96], BF16)
        s_cw = A("s_cw", [128, NDT * 3])
        s_cb = A("s_cb", [128, NDT])
        s_dtbv = A("s_dtbv", [128, NDT])
        s_dp = A("s_dp", [128, NDT])
        s_b1 = A("s_b1", [128, NF])
        s_b2 = A("s_b2", [128, ND])
        s_an = A("s_an", [128, NDT * NS])
        s_state = A("s_state", [128, NDT * NS])
        state_bufs = [P.buf() for _ in range(NDT * NS)]
        s_stash = A("s_stash", [128, NDT * 2])
        stash_bufs = [P.buf() for _ in range(NDT)]
        s_eps = A("s_eps", [128, 1])
        s_one = A("s_one", [128, 1])
        scr0 = A("scr0", [128, TC])
        scr1 = A("scr1", [128, TC])
        vtmp = A("vtmp", [32, 128])
        atmp = A("atmp", [128, NDT * NS])
        aex = A("aex", [128, NDT * NS])

        p = [A(f"ps{i}", [128, 512], psum=True) for i in range(6)]
        ptr = [A(f"ptr{i}", [128, 128], psum=True) for i in range(2)]

        sems = {}
        for e in ENGS:
            sems[(e, -1)] = es.enter_context(nc.semaphore(f"sem_{e}"))
            for ln in range(DMA_LANES[e]):
                sems[(e, ln)] = es.enter_context(
                    nc.semaphore(f"sem_{e}_d{ln}"))
        sbar = es.enter_context(nc.semaphore("sem_bar"))

        left = nc._sbuf_addr_for_side("left")
        right = nc._sbuf_addr_for_side("right")
        MB = (left + 31) & ~31
        assert MB + 192 * KB <= right, (MB, right)
        mmap = []
        PRO = frozenset({-1})
        ALL = frozenset(range(9))

        def M(name, shape, dtype, off_kb, phases):
            return Manual(nc, P, mmap, name, shape, dtype,
                          MB + int(off_kb * KB), frozenset(phases))

        def dma(q, out_ap, in_ap, wb=(), rb=(), tr=False, extra=()):
            P.op(q, lambda e, o=out_ap, i=in_ap, t=tr:
                 e.dma_start(out=o, in_=i, transpose=t),
                 r=rb, w=wb, dma=True, extra=extra, dq=q + ("tr" if tr else ""))

        def mm(pa, pb, la, lb, ra, rb_, start, stop):
            P.op("PE", lambda e, o=pa, lt=la, rr=ra, s1=start, s2=stop:
                 e.matmul(o, lt, rr, start=s1, stop=s2),
                 r=(lb, rb_), w=(pb,))

        def act(out_ap, ob, in_ap, ib, func, bias=None, scale=None, rextra=()):
            kw = {}
            if bias is not None:
                kw["bias"] = bias
            if scale is not None:
                kw["scale"] = scale
            P.op("ACT", lambda e, o=out_ap, i=in_ap, f=func, k=dict(kw):
                 e.activation(out=o, in_=i, func=f, **k),
                 r=(ib,) + tuple(rextra), w=(ob,))

        def acopy(out_ap, ob, in_ap, ib):
            P.op("ACT", lambda e, o=out_ap, i=in_ap: e.copy(out=o, in_=i),
                 r=(ib,), w=(ob,))

        def vcopy(out_ap, ob, in_ap, ib):
            P.op("DVE", lambda e, o=out_ap, i=in_ap: e.tensor_copy(out=o, in_=i),
                 r=(ib,), w=(ob,))

        def tt(out_ap, ob, a_ap, ab, b_ap, bb, opk):
            P.op("DVE", lambda e, o=out_ap, i0=a_ap, i1=b_ap, k=opk:
                 e.tensor_tensor(out=o, in0=i0, in1=i1, op=k),
                 r=(ab, bb), w=(ob,))

        def ts(out_ap, ob, in_ap, ib, s1, op0, rextra=()):
            P.op("DVE", lambda e, o=out_ap, i=in_ap, s=s1, k=op0:
                 e.tensor_scalar(out=o, in0=i, scalar1=s, scalar2=None, op0=k),
                 r=(ib,) + tuple(rextra), w=(ob,))

        def stt(out_ap, ob, in0_ap, i0b, s1, in1_ap, i1b, op0, op1, rextra=()):
            P.op("DVE", lambda e, o=out_ap, i0=in0_ap, s=s1, i1=in1_ap,
                 k0=op0, k1=op1:
                 e.scalar_tensor_tensor(out=o, in0=i0, scalar=s, in1=i1,
                                        op0=k0, op1=k1),
                 r=(i0b, i1b) + tuple(rextra), w=(ob,))

        def scan(out_ap, ob, a_ap, ab, b_ap, bb, init_ap, initb):
            P.op("DVE", lambda e, o=out_ap, i0=a_ap, i1=b_ap, ii=init_ap:
                 e.tensor_tensor_scan(o, i0, i1, ii, ALU.mult, ALU.add),
                 r=(ab, bb, initb), w=(ob,), fsync=True)

        def petr(p_ap, pb, in_ap, ib, nk=128):
            P.op("PE", lambda e, o=p_ap, i=in_ap, idap=s_idf.h[0:nk, 0:nk]:
                 e.transpose(o, i, idap),
                 r=(ib, s_idf.b), w=(pb,))

        dma("SP", s_idf.h[:, :], c_idf, wb=(s_idf.b,))
        dma("SP", s_idb.h[:, :], c_idb, wb=(s_idb.b,))
        dma("SP", s_onesb.h[:, :], c_onesb, wb=(s_onesb.b,))
        dma("SP", s_ones1.h[:, :], c_ones1, wb=(s_ones1.b,))
        dma("SP", s_cw.h[:, :].rearrange("p (b k) -> p b k", k=3),
            w_cw.rearrange("(b p) k -> p b k", p=128), wb=(s_cw.b,))
        stg = M("stg", [128, 32768], BF16, 0, PRO)
        stgL = P.buf()
        stgH = P.buf()
        prep = [
            (w_ip, sc_ip, 2 * D2, D, b_scip), (w_dt, sc_dt, D2, RK, b_scdt),
            (w_op, sc_op, D, D2, b_scop), (w_w1, sc_w1, DF, D, b_scw1),
            (w_w2, sc_w2, D, DF, b_scw2), (w_wg, sc_wg, H2, D, b_scwg),
            (w_wu, sc_wu, H2, D, b_scwu), (w_wd, sc_wd, D, H2, b_scwd),
        ]
        gpar = 0

        def prep_one(src, dst, R, C, dstb, stages=None):
            nonlocal gpar
            nb = R // 128
            if stages is None:
                stages = [(stg.h[:, 0:16384], stgL),
                          (stg.h[:, 16384:32768], stgH)]
            scols = int(stages[0][0].shape[-1])
            if C > scols:
                for c0 in range(0, C, scols):
                    cc = min(scols, C - c0)
                    for b0 in range(nb):
                        sap, half = stages[gpar]
                        sv = sap[:, 0:cc].rearrange("p (b c) -> p b c", b=1)
                        dma("GP", sv,
                            src.rearrange("(b p) c -> p b c", p=128)
                            [:, b0:b0 + 1, c0:c0 + cc], wb=(half,))
                        dma("ACT",
                            dst.rearrange("(b p) c -> p b c", p=128)
                            [:, b0:b0 + 1, c0:c0 + cc], sv,
                            rb=(half,), wb=(dstb,))
                        gpar ^= 1
                return
            maxb = max(1, scols // C)
            b0 = 0
            while b0 < nb:
                g = min(maxb, nb - b0)
                sap, half = stages[gpar]
                sv = sap[:, 0:g * C].rearrange("p (b c) -> p b c", b=g)
                dma("GP", sv,
                    src.rearrange("(b p) c -> p b c", p=128)[:, b0:b0 + g, :],
                    wb=(half,))
                dma("ACT",
                    dst.rearrange("(b p) c -> p b c", p=128)[:, b0:b0 + g, 0:C],
                    sv, rb=(half,), wb=(dstb,))
                gpar ^= 1
                b0 += g

        def prep_xp():
            nonlocal gpar
            half = [stgL, stgH][gpar]
            dma("GP", stg.h[0:96, gpar * 16384: gpar * 16384 + D2], w_xp,
                wb=(half,))
            dma("ACT", sc_xp, stg.h[0:96, gpar * 16384: gpar * 16384 + D2],
                rb=(half,), wb=(b_scxp,))
            gpar ^= 1

        prep_one(*prep[0])
        prep_xp()
        prep_one(*prep[1])
        vecs = [(w_cb, D2 // 128, s_cb), (w_dtb, D2 // 128, s_dtbv),
                (w_dp, D2 // 128, s_dp), (w_b2, D // 128, s_b2),
                (w_b1, DF // 128, s_b1)]
        for (src, nblk, dst) in vecs:
            dma("SP", vtmp.h[0:nblk, :], src.rearrange("(b p) -> b p", p=128),
                wb=(vtmp.b,))
            petr(ptr[0].h[:, 0:nblk], ptr[0].b, vtmp.h[0:nblk, :], vtmp.b,
                 nk=nblk)
            vcopy(dst.h[:, :], dst.b, ptr[0].h[:, 0:nblk], ptr[0].b)
        dma("SP", atmp.h[:, :].rearrange("p (b n) -> p b n", n=NS),
            w_al.rearrange("(b p) n -> p b n", p=128), wb=(atmp.b,))
        act(aex.h[:, :], aex.b, atmp.h[:, :], atmp.b, AF.Exp)
        ts(s_an.h[:, :], s_an.b, aex.h[:, :], aex.b, -1.0, ALU.mult)
        P.op("DVE", lambda e: e.memset(s_eps.h[:, :], 1e-5), w=(s_eps.b,))
        P.op("DVE", lambda e: e.memset(s_one.h[:, :], 1.0), w=(s_one.b,))
        P.op("DVE", lambda e: e.memset(s_state.h[:, :], 0.0),
             w=tuple(state_bufs))
        P.op("DVE", lambda e: e.memset(s_stash.h[:, :], 0.0),
             w=tuple(stash_bufs))

        for tc in range(NCH):
            t0 = tc * TC
            su = f"c{tc}"
            mmap.clear()

            x_fm = M(f"x_fm{su}", [128, ND * TC], F32, 64, {0, 1, 2, 3, 4, 5})
            mbc = M(f"mbc{su}", [128, TC], F32, 188, set(range(7)))
            rbc = M(f"rbc{su}", [128, TC], F32, 190, set(range(7)))
            r_mean = M(f"rmean{su}", [1, TC], F32, 96, {0, 6})
            r_e2 = M(f"re2{su}", [1, TC], F32, 98, {0, 6})
            r_tmp = M(f"rtmp{su}", [1, TC], F32, 100, {0, 6})
            r_rstd = M(f"rrstd{su}", [1, TC], F32, 102, {0, 6})
            scr2 = M(f"scr2{su}", [128, TC], F32, 104, {1, 2})
            scr3 = M(f"scr3{su}", [128, TC], F32, 106, {1, 2})

            xstg = M(f"xstg{su}", [128, ND * TC // 2], F32, 80, {0})
            ln1b = M(f"ln1b{su}", [128, ND * TC], BF16, 128, {0, 1})
            xbfA = M(f"xbfA{su}", [128, ND * TC], BF16, 136, {0})
            ipW = M(f"ipW{su}", [128, 8 * 4096], BF16, 0, {0, 1})
            ips = ipW.split(8)

            for half_i in range(2):
                for tb in range(2):
                    tb2 = half_i * 2 + tb
                    dma("SP", xstg.h[:, tb * D:(tb + 1) * D],
                        in_x[t0 + tb2 * 128: t0 + (tb2 + 1) * 128, :],
                        wb=(xstg.b,))
                for tb in range(2):
                    tb2 = half_i * 2 + tb
                    for ft in range(ND):
                        par = (tb * ND + ft) % 2
                        petr(ptr[par].h[:, :], ptr[par].b,
                             xstg.h[:, tb * D + ft * 128: tb * D + (ft + 1) * 128],
                             xstg.b)
                        vcopy(x_fm.h[:, ft * TC + tb2 * 128:
                                     ft * TC + (tb2 + 1) * 128],
                              x_fm.b, ptr[par].h[:, :], ptr[par].b)
            for kc in range(8):
                P.op("SP", lambda e, o=ipW.h[:, kc * 4096:(kc + 1) * 4096],
                     i=sc_ip[0:2 * D2, kc * 128:(kc + 1) * 128]:
                     e.dma_start(out=o, in_=i, transpose=True),
                     w=(ips[kc],), r=(b_scip,), dma=True, dq="SPtr")

            def emit_ln(srcA, outb_A, xbf_half, nb):
                for hi in range(2):
                    for ft in range(nb // 2):
                        f2 = hi * (nb // 2) + ft
                        sl = srcA.h[:, f2 * TC:(f2 + 1) * TC]
                        acopy(xbf_half.h[:, ft * TC:(ft + 1) * TC], xbf_half.b,
                              sl, srcA.b)
                        act(xbf_half.h[:, (nb // 2 + ft) * TC:
                                       (nb // 2 + ft + 1) * TC],
                            xbf_half.b, sl, srcA.b, AF.Square)
                    for ft in range(nb // 2):
                        mm(p[0].h[0:1, :], p[0].b, s_onesb.h[:, :], s_onesb.b,
                           xbf_half.h[:, ft * TC:(ft + 1) * TC], xbf_half.b,
                           hi == 0 and ft == 0, hi == 1 and ft == nb // 2 - 1)
                        mm(p[1].h[0:1, :], p[1].b, s_onesb.h[:, :], s_onesb.b,
                           xbf_half.h[:, (nb // 2 + ft) * TC:
                                      (nb // 2 + ft + 1) * TC], xbf_half.b,
                           hi == 0 and ft == 0, hi == 1 and ft == nb // 2 - 1)
                ts(r_mean.h[:, :], r_mean.b, p[0].h[0:1, :], p[0].b,
                   1.0 / (nb * 128), ALU.mult)
                ts(r_e2.h[:, :], r_e2.b, p[1].h[0:1, :], p[1].b,
                   1.0 / (nb * 128), ALU.mult)
                tt(r_tmp.h[:, :], r_tmp.b, r_mean.h[:, :], r_mean.b,
                   r_mean.h[:, :], r_mean.b, ALU.mult)
                tt(r_e2.h[:, :], r_e2.b, r_e2.h[:, :], r_e2.b,
                   r_tmp.h[:, :], r_tmp.b, ALU.subtract)
                act(r_tmp.h[:, :], r_tmp.b, r_e2.h[:, :], r_e2.b, AF.Ln,
                    bias=s_eps.h[0:1, :], rextra=(s_eps.b,))
                act(r_rstd.h[:, :], r_rstd.b, r_tmp.h[:, :], r_tmp.b, AF.Exp,
                    scale=-0.5)
                mm(p[2].h[:, :], p[2].b, s_ones1.h[:, :], s_ones1.b,
                   r_mean.h[:, :], r_mean.b, True, True)
                mm(p[3].h[:, :], p[3].b, s_ones1.h[:, :], s_ones1.b,
                   r_rstd.h[:, :], r_rstd.b, True, True)
                acopy(mbc.h[:, :], mbc.b, p[2].h[:, :], p[2].b)
                acopy(rbc.h[:, :], rbc.b, p[3].h[:, :], p[3].b)
                for ft in range(nb):
                    sl = srcA.h[:, ft * TC:(ft + 1) * TC]
                    tt(scr0.h[:, :], scr0.b, sl, srcA.b, mbc.h[:, :], mbc.b,
                       ALU.subtract)
                    tt(outb_A.h[:, ft * TC:(ft + 1) * TC], outb_A.b,
                       scr0.h[:, :], scr0.b, rbc.h[:, :], rbc.b, ALU.mult)

            emit_ln(x_fm, ln1b, xbfA, ND)
            if debug and tc == 0:
                dma("GP", dbg["dbg_xfm"], x_fm.h[:, :], rb=(x_fm.b,))
                dma("GP", dbg["dbg_ln1b"], ln1b.h[:, :], rb=(ln1b.b,))

            siluzb = M(f"siluzb{su}", [128, NDT * TC], BF16, 112, {1, 2, 3})
            xp2b = M(f"xp2b{su}", [128, NDT * TC], BF16, 136, {1, 2, 3})
            xpt = [M(f"xpt{i}{su}", [128, 516], F32, 152 + i * 2.0625, {1})
                   for i in range(2)]
            for ft in range(32):
                pb = p[ft % 2]
                for kc in range(8):
                    mm(pb.h[:, :], pb.b,
                       ipW.h[:, kc * 4096 + ft * 128: kc * 4096 + (ft + 1) * 128],
                       ips[kc], ln1b.h[:, kc * TC:(kc + 1) * TC], ln1b.b,
                       kc == 0, kc == 7)
                if ft < NDT:
                    dtile = ft
                    par = dtile % 2
                    xt = xpt[par]
                    sA = [scr0, scr2][par]
                    sB = [scr1, scr3][par]
                    vcopy(xt.h[:, 0:2], xt.b,
                          s_stash.h[:, dtile * 2: dtile * 2 + 2],
                          stash_bufs[dtile])
                    acopy(xt.h[:, 2:514], xt.b, pb.h[:, :], pb.b)
                    vcopy(s_stash.h[:, dtile * 2: dtile * 2 + 2],
                          stash_bufs[dtile], xt.h[:, 512:514], xt.b)
                    k0 = s_cw.h[:, dtile * 3 + 0: dtile * 3 + 1]
                    k1 = s_cw.h[:, dtile * 3 + 1: dtile * 3 + 2]
                    k2 = s_cw.h[:, dtile * 3 + 2: dtile * 3 + 3]
                    ts(sA.h[:, :], sA.b, xt.h[:, 0:512], xt.b, k0, ALU.mult,
                       rextra=(s_cw.b,))
                    stt(sB.h[:, :], sB.b, xt.h[:, 1:513], xt.b, k1,
                        sA.h[:, :], sA.b, ALU.mult, ALU.add, rextra=(s_cw.b,))
                    stt(sA.h[:, :], sA.b, xt.h[:, 2:514], xt.b, k2,
                        sB.h[:, :], sB.b, ALU.mult, ALU.add, rextra=(s_cw.b,))
                    act(xp2b.h[:, dtile * TC:(dtile + 1) * TC], xp2b.b,
                        sA.h[:, :], sA.b, AF.Silu,
                        bias=s_cb.h[:, dtile:dtile + 1], rextra=(s_cb.b,))
                else:
                    zt = ft - NDT
                    act(siluzb.h[:, zt * TC:(zt + 1) * TC], siluzb.b,
                        pb.h[:, :], pb.b, AF.Silu)
            if debug and tc == 0:
                dma("GP", dbg["dbg_xp2b"], xp2b.h[:, :], rb=(xp2b.b,))
                dma("GP", dbg["dbg_siluz"], siluzb.h[:, :], rb=(siluzb.b,))

            dtb16 = M(f"dtb16{su}", [128, NDT * TC], BF16, 32, {2, 3})
            dtb16_s = dtb16.split(NDT)
            dtinb = M(f"dtinb{su}", [RK, TC], BF16, 98, {2})
            bcs = M(f"bcs{su}", [32, TC], BF16, 99, {2, 3})
            dtWc = M(f"dtWc{su}", [128, D2], BF16, 88, {2})
            if tc == 0:
                for kc in range(NDT):
                    dma("SP", s_xpW.h[:, kc * 96:(kc + 1) * 96],
                        sc_xp[0:96, kc * 128:(kc + 1) * 128], wb=(s_xpW.b,),
                        rb=(b_scxp,), tr=True)
            dma("SP", dtWc.h[:, :], sc_dt[0:D2, 0:128], wb=(dtWc.b,), rb=(b_scdt,), tr=True)
            if tc == 0:
                dstg = [M(f"dstg{i}", [128, 2048], BF16, 176 + i * 4,
                          {2, 3, 3.5}) for i in range(2)]
                dstages = [(dstg[0].h[:, :], dstg[0].b),
                           (dstg[1].h[:, :], dstg[1].b)]

                def _prep_pairs():
                    gp = [0]
                    scols = 2048
                    for (srcw, dstw, R, C, dstb) in prep[2:]:
                        nb = R // 128
                        for c0 in range(0, C, scols):
                            cc = min(scols, C - c0)
                            gmax = max(1, scols // C) if C <= scols else 1
                            b0 = 0
                            while b0 < nb:
                                g = min(gmax, nb - b0)
                                sap, half = dstages[gp[0]]

                                def emit(srcw=srcw, dstw=dstw, b0=b0, g=g,
                                         c0=c0, cc=cc, sap=sap, half=half,
                                         dstb=dstb):
                                    sv = sap[:, 0:g * cc].rearrange(
                                        "p (b c) -> p b c", b=g)
                                    dma("GP", sv, srcw.rearrange(
                                        "(b p) c -> p b c", p=128)
                                        [:, b0:b0 + g, c0:c0 + cc], wb=(half,))
                                    dma("ACT", dstw.rearrange(
                                        "(b p) c -> p b c", p=128)
                                        [:, b0:b0 + g, c0:c0 + cc], sv,
                                        rb=(half,), wb=(dstb,))
                                yield emit
                                gp[0] ^= 1
                                b0 += g
                            if C <= scols:
                                break

                prep_iter = _prep_pairs()
            else:
                prep_iter = iter(())
            for kc in range(NDT):
                mm(p[5].h[0:96, :], p[5].b, s_xpW.h[:, kc * 96:(kc + 1) * 96],
                   s_xpW.b, xp2b.h[:, kc * TC:(kc + 1) * TC], xp2b.b,
                   kc == 0, kc == NDT - 1)
            acopy(dtinb.h[:, :], dtinb.b, p[5].h[0:RK, :], p[5].b)
            acopy(bcs.h[:, :], bcs.b, p[5].h[RK:96, :], p[5].b)
            for dtile in range(NDT):
                pb = p[dtile % 2]
                mm(pb.h[:, :], pb.b,
                   dtWc.h[0:RK, dtile * 128:(dtile + 1) * 128], dtWc.b,
                   dtinb.h[:, :], dtinb.b, True, True)
                par = dtile % 2
                sA = [scr0, scr2][par]
                sB = [scr1, scr3][par]
                ts(sA.h[:, :], sA.b, pb.h[:, :], pb.b,
                   s_dtbv.h[:, dtile:dtile + 1], ALU.add, rextra=(s_dtbv.b,))
                act(sB.h[:, :], sB.b, sA.h[:, :], sA.b, AF.Abs)
                act(sB.h[:, :], sB.b, sB.h[:, :], sB.b, AF.Exp, scale=-1.0)
                act(sB.h[:, :], sB.b, sB.h[:, :], sB.b, AF.Ln,
                    bias=s_one.h[:, :], rextra=(s_one.b,))
                stt(dtb16.h[:, dtile * TC:(dtile + 1) * TC], dtb16_s[dtile],
                    sA.h[:, :], sA.b, 0.0, sB.h[:, :], sB.b, ALU.max, ALU.add)
            if debug and tc == 0:
                dma("GP", dbg["dbg_dtb"], dtb16.h[:, :], rb=(dtb16.b,))
                dma("GP", dbg["dbg_dtinb"], dtinb.h[:, :], rb=(dtinb.b,))
                dma("GP", dbg["dbg_bcs"], bcs.h[:, :], rb=(bcs.b,))

            Bbc = M(f"Bbc{su}", [128, NS * TC], BF16, 0, {3})
            Bbc_s = Bbc.split(NS)
            Cbc = M(f"Cbc{su}", [128, NS * TC], BF16, 16, {3})
            Cbc_s = Cbc.split(NS)
            yf = M(f"yf{su}", [128, NDT * TC], BF16, 48, {3, 4})
            s_ohD = M(f"soh{su}", [32, 32 * 128], BF16, 80, {3})
            DEPTH = 4
            a_t = [M(f"a{i}{su}", [128, TC], F32, 80 + i * 2, {3.5})
                   for i in range(DEPTH)]
            b_t = [M(f"b{i}{su}", [128, TC], BF16, 88 + i * 1, {3.5})
                   for i in range(DEPTH)]
            h_t = [M(f"h{i}{su}", [128, TC], F32, 168 + i * 2, {3.5})
                   for i in range(DEPTH)]
            ch_t = [M(f"ch{i}{su}", [128, TC], BF16, 184 + i * 1, {3.5})
                    for i in range(DEPTH)]
            hb_t = [M(f"hb{i}{su}", [128, TC], BF16, 108 + i * 1, {3.5})
                    for i in range(DEPTH)]
            u_t = [M(f"u{i}{su}", [128, TC], BF16, [96, 102][i], {3.5})
                     for i in range(2)]
            xpd_t = [M(f"xpd{i}{su}", [128, TC], BF16, 100 + i, {3.5})
                     for i in range(2)]
            dma("SP", s_ohD.h[:, :], c_oh, wb=(s_ohD.b,))
            for n in range(NS):
                mm(p[2].h[:, :], p[2].b, s_ohD.h[:, n * 128:(n + 1) * 128],
                   s_ohD.b, bcs.h[:, :], bcs.b, True, True)
                acopy(Bbc.h[:, n * TC:(n + 1) * TC], Bbc_s[n], p[2].h[:, :],
                      p[2].b)
                mm(p[3].h[:, :], p[3].b,
                   s_ohD.h[:, (NS + n) * 128:(NS + n + 1) * 128], s_ohD.b,
                   bcs.h[:, :], bcs.b, True, True)
                acopy(Cbc.h[:, n * TC:(n + 1) * TC], Cbc_s[n], p[3].h[:, :],
                      p[3].b)
            pend_save = [None]
            pend_ch = []

            def emit_save():
                if pend_save[0] is not None:
                    (sidx, spar) = pend_save[0]
                    acopy(s_state.h[:, sidx:sidx + 1], state_bufs[sidx],
                          h_t[spar].h[:, TC - 1:TC], h_t[spar].b)
                    acopy(hb_t[spar].h[:, :], hb_t[spar].b,
                          h_t[spar].h[:, :], h_t[spar].b)
                    pend_save[0] = None

            def emit_ch(upto):
                while pend_ch and pend_ch[0][0] <= upto:
                    (cidx, cpar, cn, cpyp) = pend_ch.pop(0)
                    tt(ch_t[cpar].h[:, :], ch_t[cpar].b, hb_t[cpar].h[:, :],
                       hb_t[cpar].b, Cbc.h[:, cn * TC:(cn + 1) * TC], Cbc_s[cn],
                       ALU.mult)
                    mm(cpyp.h[:, :], cpyp.b, s_idb.h[:, :], s_idb.b,
                       ch_t[cpar].h[:, :], ch_t[cpar].b, False, cn == NS - 1)

            for dtile in range(NDT):
                dsl = slice(dtile * TC, (dtile + 1) * TC)
                dpar = dtile % 2
                pyp = p[4 + dpar]
                tt(u_t[dpar].h[:, :], u_t[dpar].b, dtb16.h[:, dsl],
                   dtb16_s[dtile], xp2b.h[:, dsl], xp2b.b, ALU.mult)
                act(xpd_t[dpar].h[:, :], xpd_t[dpar].b, xp2b.h[:, dsl],
                    xp2b.b, AF.Copy, scale=s_dp.h[:, dtile:dtile + 1],
                    rextra=(s_dp.b,))
                mm(pyp.h[:, :], pyp.b, s_idb.h[:, :], s_idb.b,
                   xpd_t[dpar].h[:, :], xpd_t[dpar].b, True, False)
                for n in range(NS):
                    idx = dtile * NS + n
                    par = idx % DEPTH
                    act(a_t[par].h[:, :], a_t[par].b, dtb16.h[:, dsl],
                        dtb16_s[dtile], AF.Exp, scale=s_an.h[:, idx:idx + 1],
                        rextra=(s_an.b,))
                    tt(b_t[par].h[:, :], b_t[par].b, u_t[dpar].h[:, :],
                       u_t[dpar].b, Bbc.h[:, n * TC:(n + 1) * TC], Bbc_s[n],
                       ALU.mult)
                    scan(h_t[par].h[:, :], h_t[par].b, a_t[par].h[:, :],
                         a_t[par].b, b_t[par].h[:, :], b_t[par].b,
                         s_state.h[:, idx:idx + 1], state_bufs[idx])
                    emit_save()
                    pend_save[0] = (idx, par)
                    nxt = next(prep_iter, None)
                    if nxt is not None:
                        nxt()
                    pend_ch.append((idx, par, n, pyp))
                    emit_ch(idx - 2)
                emit_ch(dtile * NS + NS - 1)
                tt(yf.h[:, dsl], yf.b, pyp.h[:, :], pyp.b,
                   siluzb.h[:, dsl], siluzb.b, ALU.mult)
            emit_save()
            if debug and tc == 0:
                dma("GP", dbg["dbg_yf"], yf.h[:, :], rb=(yf.b,))
                dma("GP", dbg["dbg_state"], s_state.h[:, :], rb=(s_state.b,))

            opW = M(f"opW{su}", [128, NDT * 1024], BF16, 0, {4})
            ops_ = opW.split(NDT)
            t1f = M(f"t1f{su}", [128, ND * TC], F32, 96, {4, 5})
            t1b = M(f"t1b{su}", [128, ND * TC], BF16, 128, {4, 5})
            t1b_s = t1b.split(ND)
            for kc in range(NDT):
                P.op("SP", lambda e, o=opW.h[:, kc * 1024:(kc + 1) * 1024],
                     i=sc_op[0:D, kc * 128:(kc + 1) * 128]:
                     e.dma_start(out=o, in_=i, transpose=True),
                     w=(ops_[kc],), r=(b_scop,), dma=True, dq="SPtr")
            for ft in range(ND):
                pb = p[ft % 2]
                for kc in range(NDT):
                    mm(pb.h[:, :], pb.b,
                       opW.h[:, kc * 1024 + ft * 128: kc * 1024 + (ft + 1) * 128],
                       ops_[kc], yf.h[:, kc * TC:(kc + 1) * TC], yf.b,
                       kc == 0, kc == NDT - 1)
                par = ft % 2
                fsl = slice(ft * TC, (ft + 1) * TC)
                sA = [scr0, scr1][par]
                tt(sA.h[:, :], sA.b, x_fm.h[:, fsl], x_fm.b,
                   mbc.h[:, :], mbc.b, ALU.subtract)
                tt(sA.h[:, :], sA.b, sA.h[:, :], sA.b, rbc.h[:, :], rbc.b,
                   ALU.mult)
                tt(t1f.h[:, fsl], t1f.b, pb.h[:, :], pb.b, sA.h[:, :], sA.b,
                   ALU.add)
                acopy(t1b.h[:, fsl], t1b_s[ft], t1f.h[:, fsl], t1f.b)
            if debug and tc == 0:
                dma("GP", dbg["dbg_t1f"], t1f.h[:, :], rb=(t1f.b,))

            w1W = M(f"w1W{su}", [128, 8 * 4096], BF16, 0, {5})
            w1s = w1W.split(8)
            hgelA = M(f"hgelA{su}", [128, NF * TC // 2], BF16, 112, {5})
            hgelA_s = hgelA.split(NF // 2)
            hgelB = M(f"hgelB{su}", [128, NF * TC // 2], BF16, 80, {5})
            hgelB_s = hgelB.split(NF // 2)
            x1 = M(f"x1{su}", [128, ND * TC], F32, 152, {5, 6, 7, 8})
            for kc in range(8):
                P.op("SP", lambda e, o=w1W.h[:, kc * 4096:(kc + 1) * 4096],
                     i=sc_w1[0:DF, kc * 128:(kc + 1) * 128]:
                     e.dma_start(out=o, in_=i, transpose=True),
                     w=(w1s[kc],), r=(b_scw1,), dma=True, dq="SPtr")

            def hgel_ap(kb):
                if kb < 16:
                    return hgelA.h[:, kb * TC:(kb + 1) * TC], hgelA_s[kb]
                return hgelB.h[:, (kb - 16) * TC:(kb - 15) * TC], hgelB_s[kb - 16]

            for ft in range(NF):
                pb = p[ft % 2]
                for kc in range(8):
                    mm(pb.h[:, :], pb.b,
                       w1W.h[:, kc * 4096 + ft * 128: kc * 4096 + (ft + 1) * 128],
                       w1s[kc], t1b.h[:, kc * TC:(kc + 1) * TC], t1b_s[kc],
                       kc == 0, kc == 7)
                ga, gb = hgel_ap(ft)
                act(ga, gb, pb.h[:, :], pb.b, AF.Gelu,
                    bias=s_b1.h[:, ft:ft + 1], rextra=(s_b1.b,))
            for kb in range(32):
                P.op("SP", lambda e, o=w1W.h[:, kb * 1024:(kb + 1) * 1024],
                     i=sc_w2[0:D, kb * 128:(kb + 1) * 128]:
                     e.dma_start(out=o, in_=i, transpose=True),
                     w=(w1s[kb // 4],), r=(b_scw2,), dma=True, dq="SPtr")
            for ft in range(ND):
                pb = p[2 + ft % 2]
                for kb in range(32):
                    ga, gb = hgel_ap(kb)
                    mm(pb.h[:, :], pb.b,
                       w1W.h[:, kb * 1024 + ft * 128: kb * 1024 + (ft + 1) * 128],
                       w1s[kb // 4], ga, gb, kb == 0, kb == 31)
                par = ft % 2
                fsl = slice(ft * TC, (ft + 1) * TC)
                sA = [scr0, scr1][par]
                stt(sA.h[:, :], sA.b, pb.h[:, :], pb.b,
                    s_b2.h[:, ft:ft + 1], t1f.h[:, fsl], t1f.b,
                    ALU.add, ALU.add, rextra=(s_b2.b,))
                tt(x1.h[:, fsl], x1.b, sA.h[:, :], sA.b, x_fm.h[:, fsl], x_fm.b,
                   ALU.add)
            if debug and tc == 0:
                dma("GP", dbg["dbg_x1"], x1.h[:, :], rb=(x1.b,))

            ln2b = M(f"ln2b{su}", [128, ND * TC], BF16, 128, {6, 7})
            xbfG = M(f"xbfG{su}", [128, ND * TC], BF16, 136, {6})
            emit_ln(x1, ln2b, xbfG, ND)
            if debug and tc == 0:
                dma("GP", dbg["dbg_ln2b"], ln2b.h[:, :], rb=(ln2b.b,))

            wgW = M(f"wgW{su}", [128, 8 * H2], BF16, 0, {7})
            wgs = wgW.split(8)
            wuW = M(f"wuW{su}", [128, 8 * H2], BF16, 48, {7})
            wus = wuW.split(8)
            hs = M(f"hs{su}", [128, NH * TC], BF16, 96, {7, 8})
            hs_s = hs.split(NH)
            for kc in range(8):
                P.op("SP", lambda e, o=wgW.h[:, kc * H2:(kc + 1) * H2],
                     i=sc_wg[0:H2, kc * 128:(kc + 1) * 128]:
                     e.dma_start(out=o, in_=i, transpose=True),
                     w=(wgs[kc],), r=(b_scwg,), dma=True, dq="SPtr")
                P.op("SP", lambda e, o=wuW.h[:, kc * H2:(kc + 1) * H2],
                     i=sc_wu[0:H2, kc * 128:(kc + 1) * 128]:
                     e.dma_start(out=o, in_=i, transpose=True),
                     w=(wus[kc],), r=(b_scwu,), dma=True, dq="SPtr")
            for ft in range(NH):
                par = ft % 2
                pg = p[par]
                pu = p[2 + par]
                for kc in range(8):
                    mm(pg.h[:, :], pg.b,
                       wgW.h[:, kc * H2 + ft * 128: kc * H2 + (ft + 1) * 128],
                       wgs[kc], ln2b.h[:, kc * TC:(kc + 1) * TC], ln2b.b,
                       kc == 0, kc == 7)
                for kc in range(8):
                    mm(pu.h[:, :], pu.b,
                       wuW.h[:, kc * H2 + ft * 128: kc * H2 + (ft + 1) * 128],
                       wus[kc], ln2b.h[:, kc * TC:(kc + 1) * TC], ln2b.b,
                       kc == 0, kc == 7)
                sA = [scr0, scr1][par]
                act(sA.h[:, :], sA.b, pg.h[:, :], pg.b, AF.Silu, scale=QSCALE)
                tt(hs.h[:, ft * TC:(ft + 1) * TC], hs.b, sA.h[:, :], sA.b,
                   pu.h[:, :], pu.b, ALU.mult)
            if debug and tc == 0:
                dma("GP", dbg["dbg_hs"], hs.h[:, :], rb=(hs.b,))

            wdW = M(f"wdW{su}", [128, NH * 1024], BF16, 0, {8})
            wds = wdW.split(NH)
            outF = M(f"outF{su}", [128, ND * TC], F32, 64, {8})
            outtm = M(f"outtm{su}", [128, 4 * D], F32, 80, {8})
            for kb in range(NH):
                P.op("SP", lambda e, o=wdW.h[:, kb * 1024:(kb + 1) * 1024],
                     i=sc_wd[0:D, kb * 128:(kb + 1) * 128]:
                     e.dma_start(out=o, in_=i, transpose=True),
                     w=(wds[kb],), r=(b_scwd,), dma=True, dq="SPtr")
            for ft in range(ND):
                pb = p[ft % 2]
                for kb in range(NH):
                    mm(pb.h[:, :], pb.b,
                       wdW.h[:, kb * 1024 + ft * 128: kb * 1024 + (ft + 1) * 128],
                       wds[kb], hs.h[:, kb * TC:(kb + 1) * TC], hs_s[kb],
                       kb == 0, kb == NH - 1)
                fsl = slice(ft * TC, (ft + 1) * TC)
                stt(outF.h[:, fsl], outF.b, pb.h[:, :], pb.b,
                    QSCALE * QSCALE, x1.h[:, fsl], x1.b, ALU.mult, ALU.add)
            for tb in range(4):
                for ft in range(ND):
                    par = (tb * ND + ft) % 2
                    petr(ptr[par].h[:, :], ptr[par].b,
                         outF.h[:, ft * TC + tb * 128: ft * TC + (tb + 1) * 128],
                         outF.b)
                    vcopy(outtm.h[:, tb * D + ft * 128: tb * D + (ft + 1) * 128],
                          outtm.b, ptr[par].h[:, :], ptr[par].b)
            for tb in range(4):
                dma("GP", out_y[t0 + tb * 128: t0 + (tb + 1) * 128, :],
                    outtm.h[:, tb * D:(tb + 1) * D], rb=(outtm.b,))

        P.barrier()
        P.assign_values()
        with nc.Block() as blk:
            blk.sync(lambda e: P.emit_engine("SP", e, sems, sbar))
            blk.tensor(lambda e: P.emit_engine("PE", e, sems, sbar))
            blk.scalar(lambda e: P.emit_engine("ACT", e, sems, sbar))
            blk.vector(lambda e: P.emit_engine("DVE", e, sems, sbar))
            blk.gpsimd(lambda e: P.emit_engine("GP", e, sems, sbar))

    return nc


def _consts():
    oh = np.zeros((32, 32 * 128), np.float32)
    for j in range(32):
        oh[j, j * 128:(j + 1) * 128] = 1.0
    return {
        "c_idf": np.eye(128, dtype=np.float32),
        "c_idb": np.eye(128, dtype=np.float32).astype(ml_dtypes.bfloat16),
        "c_onesb": np.ones((128, 1), np.float32).astype(ml_dtypes.bfloat16),
        "c_ones1": np.ones((1, 128), np.float32),
        "c_oh": oh.astype(ml_dtypes.bfloat16),
    }


_NAMES = [
    ("w_ip", "L0_attn_in_proj_w", np.float32),
    ("w_cw", "L0_attn_conv_w", np.float32),
    ("w_cb", "L0_attn_conv_b", np.float32),
    ("w_xp", "L0_attn_x_proj_w", np.float32),
    ("w_dt", "L0_attn_dt_proj_w", np.float32),
    ("w_dtb", "L0_attn_dt_proj_b", np.float32),
    ("w_al", "L0_attn_A_log", np.float32),
    ("w_dp", "L0_attn_Dp", np.float32),
    ("w_op", "L0_attn_out_proj_w", np.float32),
    ("w_w1", "L0_attn_w1", np.float32),
    ("w_b1", "L0_attn_b1", np.float32),
    ("w_w2", "L0_attn_w2", np.float32),
    ("w_b2", "L0_attn_b2", np.float32),
    ("w_wg", "L0_ffn_w_gate", np.int32),
    ("w_wu", "L0_ffn_w_up", np.int32),
    ("w_wd", "L0_ffn_w_down", np.int32),
]


def kernel(**inputs):
    nc = build_nc()
    base = dict(_consts())
    for (k, src, dt) in _NAMES:
        base[k] = np.ascontiguousarray(np.asarray(inputs[src], dt))
    x = np.ascontiguousarray(np.asarray(inputs["L0_attn_x"], np.float32))
    in_maps = [dict(base, in_x=np.ascontiguousarray(x[b])) for b in range(8)]
    res = run_bass_kernel_spmd(nc, in_maps, core_ids=list(range(8)))
    return np.stack(
        [np.asarray(res.results[b]["out_y"], np.float32) for b in range(8)],
        axis=0)


# version 35
# speedup vs baseline: 1.0313x; 1.0020x over previous
import contextlib
import numpy as np
import ml_dtypes
import concourse.bass as bass
import concourse.mybir as mybir
from concourse.bass_utils import run_bass_kernel_spmd

F32 = mybir.dt.float32
BF16 = mybir.dt.bfloat16
I32 = mybir.dt.int32
AF = mybir.ActivationFunctionType
ALU = mybir.AluOpType

D = 1024
D2 = 2048
L = 2048
TC = 512
NCH = L // TC
NDT = D2 // 128
ND = D // 128
NS = 16
RK = 64
DF = 4096
NF = DF // 128
H2 = 3072
NH = H2 // 128
QSCALE = 1.0 / 127.0
KB = 1024

ENGS = ("SP", "PE", "ACT", "DVE", "GP")
USE_DWAITS = False


class _Op:
    __slots__ = ("eng", "emit", "dma", "inc", "val", "deps", "seq", "bar",
                 "own_wait", "bar_target", "snapshot", "fsync", "dq", "skey",
                 "dwaits")

    def __init__(self):
        self.bar = False
        self.inc = False
        self.val = None
        self.deps = set()
        self.fsync = False
        self.dq = None


class _Buf:
    __slots__ = ("writes", "readers", "alias_prev")

    def __init__(self):
        self.writes = []
        self.readers = []
        self.alias_prev = []


DMA_LANES = {"SP": 4, "GP": 2, "ACT": 2, "DVE": 1, "PE": 1}


class Prog:
    def __init__(self):
        self.ops = []
        self.bufs = []
        self.nbar = 0
        self.last_per_engine = {e: None for e in ENGS}
        self.dma_count = {e: 0 for e in ENGS}
        self.lane_counts = {}

    def buf(self):
        b = _Buf()
        self.bufs.append(b)
        return b

    def _needs(self, d, o):
        if d is o:
            return False
        if d.fsync:
            return True
        if d.eng == o.eng:
            if d.dma and o.dma:
                return d.dq != o.dq
            if (not d.dma) and (not o.dma):
                return False
            return True
        return True

    def op(self, eng, emit, r=(), w=(), dma=False, extra=(), fsync=False,
           dq=None):
        o = _Op()
        o.eng = eng
        o.emit = emit
        o.dma = dma
        o.inc = dma
        o.fsync = fsync
        o.dq = dq if dq is not None else (eng if dma else None)
        o.dwaits = {}
        for d in o.deps:
            if d.dma:
                o.dwaits[d.skey] = self.lane_counts[d.skey]
        if dma:
            lane = self.dma_count[eng] % DMA_LANES[eng]
            self.dma_count[eng] += 1
            o.skey = (eng, lane)
            self.lane_counts[o.skey] = self.lane_counts.get(o.skey, 0) + 16
        else:
            o.skey = (eng, -1)
        for b in r:
            for d in b.writes:
                if self._needs(d, o):
                    o.deps.add(d)
        for b in w:
            for d in b.writes:
                if self._needs(d, o):
                    o.deps.add(d)
            for d in b.readers:
                if self._needs(d, o):
                    o.deps.add(d)
            for ob in b.alias_prev:
                for d in ob.writes + ob.readers:
                    if self._needs(d, o):
                        o.deps.add(d)
        for d in extra:
            if self._needs(d, o):
                o.deps.add(d)
        for d in o.deps:
            d.inc = True
        for b in r:
            b.readers.append(o)
        for b in w:
            b.writes = [o]
            b.readers = []
            b.alias_prev = []
        o.seq = len(self.ops)
        self.ops.append(o)
        self.last_per_engine[eng] = o
        return o

    def barrier(self):
        self.nbar += 1
        for e in ENGS:
            o = _Op()
            o.eng = e
            o.emit = None
            o.dma = False
            o.bar = True
            last = self.last_per_engine[e]
            if last is not None and not last.bar:
                last.inc = True
            o.own_wait = None
            o.bar_target = 5 * self.nbar
            o.seq = len(self.ops)
            self.ops.append(o)
            self.last_per_engine[e] = o
        for b in self.bufs:
            b.writes = []
            b.readers = []

    def assign_values(self):
        cnt = {}
        for o in self.ops:
            if o.bar:
                o.snapshot = dict(cnt)
            elif o.inc:
                cnt[o.skey] = cnt.get(o.skey, 0) + (16 if o.dma else 1)
                o.val = cnt[o.skey]

    def emit_engine(self, eng, engobj, sems, sbar):
        waited = {}
        for o in self.ops:
            if o.eng != eng:
                continue
            if o.bar:
                for k, v in o.snapshot.items():
                    if k[0] == eng and waited.get(k, 0) < v:
                        engobj.wait_ge(sems[k], v)
                        waited[k] = v
                engobj.sem_inc(sbar, 1)
                engobj.wait_ge(sbar, o.bar_target)
                for k, v in o.snapshot.items():
                    waited[k] = max(waited.get(k, 0), v)
                continue
            merged = dict(o.dwaits) if USE_DWAITS else {}
            for dp in o.deps:
                if USE_DWAITS and dp.dma:
                    continue
                assert dp.val is not None
                if merged.get(dp.skey, 0) < dp.val:
                    merged[dp.skey] = dp.val
            for s, v in merged.items():
                if waited.get(s, 0) < v:
                    engobj.wait_ge(sems[s], v)
                    waited[s] = v
            inst = o.emit(engobj)
            if o.inc:
                inst.then_inc(sems[o.skey], 16 if o.dma else 1)


class Alloc:

    def __init__(self, nc, P, name, shape, dtype, psum=False):
        self.cm = (nc.psum_tensor if psum else nc.sbuf_tensor)(name, shape, dtype)
        self.h = self.cm.__enter__()
        self.b = P.buf()


_DT_BYTES = {F32: 4, BF16: 2, I32: 4}


class Manual:

    seq = [0]
    registry = []

    def __init__(self, nc, P, mmap, name, shape, dtype, off, phases):
        nbytes = int(np.prod(shape[1:])) * _DT_BYTES[dtype]
        for (o2, n2, ph2, nm2) in mmap:
            if phases & ph2 and off < o2 + n2 and o2 < off + nbytes:
                raise AssertionError(
                    f"sbuf overlap {name}@[{off},{off+nbytes}) ph{phases} vs "
                    f"{nm2}@[{o2},{o2+n2}) ph{ph2}")
        mmap.append((off, nbytes, set(phases), name))
        Manual.seq[0] += 1
        self.h = nc.alloc_sbuf_tensor_at(
            f"{name}_{Manual.seq[0]}", shape, dtype, offset=off)
        self.b = P.buf()
        lo, hi = off, off + nbytes
        keep = []
        for (l2, h2, b2) in Manual.registry:
            if l2 < hi and lo < h2:
                self.b.alias_prev.append(b2)
                if l2 >= lo and h2 <= hi:
                    continue
            keep.append((l2, h2, b2))
        keep.append((lo, hi, self.b))
        Manual.registry = keep
        self._lo, self._hi, self._P = lo, hi, P

    def split(self, n):
        bufs = [self._P.buf() for _ in range(n)]
        sz = (self._hi - self._lo) // n
        Manual.registry = [e for e in Manual.registry if e[2] is not self.b]
        for i, b in enumerate(bufs):
            b.alias_prev = list(self.b.alias_prev)
            Manual.registry.append(
                (self._lo + i * sz, self._lo + (i + 1) * sz, b))
        return bufs


def build_nc(debug=False):
    nc = bass.Bass("TRN2", target_bir_lowering=False, debug=False)
    P = Prog()
    Manual.registry = []

    d_in = {}

    def din(name, shape, dt):
        d_in[name] = nc.dram_tensor(name, shape, dt, kind="ExternalInput").ap()
        return d_in[name]

    in_x = din("in_x", [L, D], F32)
    w_ip = din("w_ip", [2 * D2, D], F32)
    w_cw = din("w_cw", [D2, 3], F32)
    w_cb = din("w_cb", [D2], F32)
    w_xp = din("w_xp", [RK + 2 * NS, D2], F32)
    w_dt = din("w_dt", [D2, RK], F32)
    w_dtb = din("w_dtb", [D2], F32)
    w_al = din("w_al", [D2, NS], F32)
    w_dp = din("w_dp", [D2], F32)
    w_op = din("w_op", [D, D2], F32)
    w_w1 = din("w_w1", [DF, D], F32)
    w_b1 = din("w_b1", [DF], F32)
    w_w2 = din("w_w2", [D, DF], F32)
    w_b2 = din("w_b2", [D], F32)
    w_wg = din("w_wg", [H2, D], I32)
    w_wu = din("w_wu", [H2, D], I32)
    w_wd = din("w_wd", [D, H2], I32)
    c_idf = din("c_idf", [128, 128], F32)
    c_idb = din("c_idb", [128, 128], BF16)
    c_onesb = din("c_onesb", [128, 1], BF16)
    c_ones1 = din("c_ones1", [1, 128], F32)
    c_oh = din("c_oh", [32, 32 * 128], BF16)

    out_y = nc.dram_tensor("out_y", [L, D], F32, kind="ExternalOutput").ap()
    dbg = {}
    if debug:
        for nm, shape, dt in [
                ("dbg_xfm", [128, ND * TC], F32), ("dbg_ln1b", [128, ND * TC], BF16),
                ("dbg_xp2b", [128, NDT * TC], BF16), ("dbg_siluz", [128, NDT * TC], BF16),
                ("dbg_dtb", [128, NDT * TC], BF16), ("dbg_dtinb", [RK, TC], BF16),
                ("dbg_bcs", [32, TC], BF16), ("dbg_yf", [128, NDT * TC], BF16),
                ("dbg_t1f", [128, ND * TC], F32), ("dbg_x1", [128, ND * TC], F32),
                ("dbg_ln2b", [128, ND * TC], BF16), ("dbg_hs", [128, NH * TC], BF16),
                ("dbg_state", [128, NDT * NS], F32)]:
            dbg[nm] = nc.dram_tensor(nm, shape, dt, kind="ExternalOutput").ap()

    def scratch(name, shape):
        return nc.dram_tensor(name, shape, BF16, kind="Internal").ap()

    sc_ip = scratch("sc_ip", [2 * D2, D])
    sc_xp = scratch("sc_xp", [RK + 2 * NS, D2])
    sc_dt = scratch("sc_dt", [D2, 128])
    sc_op = scratch("sc_op", [D, D2])
    sc_w1 = scratch("sc_w1", [DF, D])
    sc_w2 = scratch("sc_w2", [D, DF])
    sc_wg = scratch("sc_wg", [H2, D])
    sc_wu = scratch("sc_wu", [H2, D])
    sc_wd = scratch("sc_wd", [D, H2])
    b_scip = P.buf(); b_scxp = P.buf(); b_scdt = P.buf(); b_scop = P.buf()
    b_scw1 = P.buf(); b_scw2 = P.buf(); b_scwg = P.buf(); b_scwu = P.buf()
    b_scwd = P.buf()

    es = contextlib.ExitStack()
    with es:
        def A(name, shape, dtype=F32, psum=False):
            return Alloc(nc, P, name, shape, dtype, psum)

        s_idf = A("s_idf", [128, 128])
        s_idb = A("s_idb", [128, 128], BF16)
        s_onesb = A("s_onesb", [128, 1], BF16)
        s_ones1 = A("s_ones1", [1, 128])
        s_xpW = A("s_xpW", [128, NDT * 96], BF16)
        s_cw = A("s_cw", [128, NDT * 3])
        s_cb = A("s_cb", [128, NDT])
        s_dtbv = A("s_dtbv", [128, NDT])
        s_dp = A("s_dp", [128, NDT])
        s_b1 = A("s_b1", [128, NF])
        s_b2 = A("s_b2", [128, ND])
        s_an = A("s_an", [128, NDT * NS])
        s_state = A("s_state", [128, NDT * NS])
        state_bufs = [P.buf() for _ in range(NDT * NS)]
        s_stash = A("s_stash", [128, NDT * 2])
        stash_bufs = [P.buf() for _ in range(NDT)]
        s_eps = A("s_eps", [128, 1])
        s_one = A("s_one", [128, 1])
        scr0 = A("scr0", [128, TC])
        scr1 = A("scr1", [128, TC])
        vtmp = A("vtmp", [32, 128])
        atmp = A("atmp", [128, NDT * NS])
        aex = A("aex", [128, NDT * NS])

        p = [A(f"ps{i}", [128, 512], psum=True) for i in range(6)]
        ptr = [A(f"ptr{i}", [128, 128], psum=True) for i in range(2)]

        sems = {}
        for e in ENGS:
            sems[(e, -1)] = es.enter_context(nc.semaphore(f"sem_{e}"))
            for ln in range(DMA_LANES[e]):
                sems[(e, ln)] = es.enter_context(
                    nc.semaphore(f"sem_{e}_d{ln}"))
        sbar = es.enter_context(nc.semaphore("sem_bar"))

        left = nc._sbuf_addr_for_side("left")
        right = nc._sbuf_addr_for_side("right")
        MB = (left + 31) & ~31
        assert MB + 192 * KB <= right, (MB, right)
        mmap = []
        PRO = frozenset({-1})
        ALL = frozenset(range(9))

        def M(name, shape, dtype, off_kb, phases):
            return Manual(nc, P, mmap, name, shape, dtype,
                          MB + int(off_kb * KB), frozenset(phases))

        def dma(q, out_ap, in_ap, wb=(), rb=(), tr=False, extra=()):
            P.op(q, lambda e, o=out_ap, i=in_ap, t=tr:
                 e.dma_start(out=o, in_=i, transpose=t),
                 r=rb, w=wb, dma=True, extra=extra, dq=q + ("tr" if tr else ""))

        def mm(pa, pb, la, lb, ra, rb_, start, stop):
            P.op("PE", lambda e, o=pa, lt=la, rr=ra, s1=start, s2=stop:
                 e.matmul(o, lt, rr, start=s1, stop=s2),
                 r=(lb, rb_), w=(pb,))

        def act(out_ap, ob, in_ap, ib, func, bias=None, scale=None, rextra=()):
            kw = {}
            if bias is not None:
                kw["bias"] = bias
            if scale is not None:
                kw["scale"] = scale
            P.op("ACT", lambda e, o=out_ap, i=in_ap, f=func, k=dict(kw):
                 e.activation(out=o, in_=i, func=f, **k),
                 r=(ib,) + tuple(rextra), w=(ob,))

        def acopy(out_ap, ob, in_ap, ib):
            P.op("ACT", lambda e, o=out_ap, i=in_ap: e.copy(out=o, in_=i),
                 r=(ib,), w=(ob,))

        def vcopy(out_ap, ob, in_ap, ib):
            P.op("DVE", lambda e, o=out_ap, i=in_ap: e.tensor_copy(out=o, in_=i),
                 r=(ib,), w=(ob,))

        def tt(out_ap, ob, a_ap, ab, b_ap, bb, opk):
            P.op("DVE", lambda e, o=out_ap, i0=a_ap, i1=b_ap, k=opk:
                 e.tensor_tensor(out=o, in0=i0, in1=i1, op=k),
                 r=(ab, bb), w=(ob,))

        def ts(out_ap, ob, in_ap, ib, s1, op0, rextra=()):
            P.op("DVE", lambda e, o=out_ap, i=in_ap, s=s1, k=op0:
                 e.tensor_scalar(out=o, in0=i, scalar1=s, scalar2=None, op0=k),
                 r=(ib,) + tuple(rextra), w=(ob,))

        def stt(out_ap, ob, in0_ap, i0b, s1, in1_ap, i1b, op0, op1, rextra=()):
            P.op("DVE", lambda e, o=out_ap, i0=in0_ap, s=s1, i1=in1_ap,
                 k0=op0, k1=op1:
                 e.scalar_tensor_tensor(out=o, in0=i0, scalar=s, in1=i1,
                                        op0=k0, op1=k1),
                 r=(i0b, i1b) + tuple(rextra), w=(ob,))

        def scan(out_ap, ob, a_ap, ab, b_ap, bb, init_ap, initb):
            P.op("DVE", lambda e, o=out_ap, i0=a_ap, i1=b_ap, ii=init_ap:
                 e.tensor_tensor_scan(o, i0, i1, ii, ALU.mult, ALU.add),
                 r=(ab, bb, initb), w=(ob,), fsync=True)

        def petr(p_ap, pb, in_ap, ib, nk=128):
            P.op("PE", lambda e, o=p_ap, i=in_ap, idap=s_idf.h[0:nk, 0:nk]:
                 e.transpose(o, i, idap),
                 r=(ib, s_idf.b), w=(pb,))

        dma("SP", s_idf.h[:, :], c_idf, wb=(s_idf.b,))
        dma("SP", s_idb.h[:, :], c_idb, wb=(s_idb.b,))
        dma("SP", s_onesb.h[:, :], c_onesb, wb=(s_onesb.b,))
        dma("SP", s_ones1.h[:, :], c_ones1, wb=(s_ones1.b,))
        dma("SP", s_cw.h[:, :].rearrange("p (b k) -> p b k", k=3),
            w_cw.rearrange("(b p) k -> p b k", p=128), wb=(s_cw.b,))
        stg = M("stg", [128, 32768], BF16, 0, PRO)
        stgL = P.buf()
        stgH = P.buf()
        prep = [
            (w_ip, sc_ip, 2 * D2, D, b_scip), (w_dt, sc_dt, D2, RK, b_scdt),
            (w_op, sc_op, D, D2, b_scop), (w_w1, sc_w1, DF, D, b_scw1),
            (w_w2, sc_w2, D, DF, b_scw2), (w_wg, sc_wg, H2, D, b_scwg),
            (w_wu, sc_wu, H2, D, b_scwu), (w_wd, sc_wd, D, H2, b_scwd),
        ]
        gpar = 0

        def prep_one(src, dst, R, C, dstb, stages=None):
            nonlocal gpar
            nb = R // 128
            if stages is None:
                stages = [(stg.h[:, 0:16384], stgL),
                          (stg.h[:, 16384:32768], stgH)]
            scols = int(stages[0][0].shape[-1])
            if C > scols:
                for c0 in range(0, C, scols):
                    cc = min(scols, C - c0)
                    for b0 in range(nb):
                        sap, half = stages[gpar]
                        sv = sap[:, 0:cc].rearrange("p (b c) -> p b c", b=1)
                        dma("GP", sv,
                            src.rearrange("(b p) c -> p b c", p=128)
                            [:, b0:b0 + 1, c0:c0 + cc], wb=(half,))
                        dma("ACT",
                            dst.rearrange("(b p) c -> p b c", p=128)
                            [:, b0:b0 + 1, c0:c0 + cc], sv,
                            rb=(half,), wb=(dstb,))
                        gpar ^= 1
                return
            maxb = max(1, scols // C)
            b0 = 0
            while b0 < nb:
                g = min(maxb, nb - b0)
                sap, half = stages[gpar]
                sv = sap[:, 0:g * C].rearrange("p (b c) -> p b c", b=g)
                dma("GP", sv,
                    src.rearrange("(b p) c -> p b c", p=128)[:, b0:b0 + g, :],
                    wb=(half,))
                dma("ACT",
                    dst.rearrange("(b p) c -> p b c", p=128)[:, b0:b0 + g, 0:C],
                    sv, rb=(half,), wb=(dstb,))
                gpar ^= 1
                b0 += g

        def prep_xp():
            nonlocal gpar
            half = [stgL, stgH][gpar]
            dma("GP", stg.h[0:96, gpar * 16384: gpar * 16384 + D2], w_xp,
                wb=(half,))
            dma("ACT", sc_xp, stg.h[0:96, gpar * 16384: gpar * 16384 + D2],
                rb=(half,), wb=(b_scxp,))
            gpar ^= 1

        prep_one(*prep[0])
        prep_xp()
        prep_one(*prep[1])
        vecs = [(w_cb, D2 // 128, s_cb), (w_dtb, D2 // 128, s_dtbv),
                (w_dp, D2 // 128, s_dp), (w_b2, D // 128, s_b2),
                (w_b1, DF // 128, s_b1)]
        for (src, nblk, dst) in vecs:
            dma("SP", vtmp.h[0:nblk, :], src.rearrange("(b p) -> b p", p=128),
                wb=(vtmp.b,))
            petr(ptr[0].h[:, 0:nblk], ptr[0].b, vtmp.h[0:nblk, :], vtmp.b,
                 nk=nblk)
            vcopy(dst.h[:, :], dst.b, ptr[0].h[:, 0:nblk], ptr[0].b)
        dma("SP", atmp.h[:, :].rearrange("p (b n) -> p b n", n=NS),
            w_al.rearrange("(b p) n -> p b n", p=128), wb=(atmp.b,))
        act(aex.h[:, :], aex.b, atmp.h[:, :], atmp.b, AF.Exp)
        ts(s_an.h[:, :], s_an.b, aex.h[:, :], aex.b, -1.0, ALU.mult)
        P.op("DVE", lambda e: e.memset(s_eps.h[:, :], 1e-5), w=(s_eps.b,))
        P.op("DVE", lambda e: e.memset(s_one.h[:, :], 1.0), w=(s_one.b,))
        P.op("DVE", lambda e: e.memset(s_state.h[:, :], 0.0),
             w=tuple(state_bufs))
        P.op("DVE", lambda e: e.memset(s_stash.h[:, :], 0.0),
             w=tuple(stash_bufs))

        for tc in range(NCH):
            t0 = tc * TC
            su = f"c{tc}"
            mmap.clear()

            x_fm = M(f"x_fm{su}", [128, ND * TC], F32, 64, {0, 1, 2, 3, 4, 5})
            mbc = M(f"mbc{su}", [128, TC], F32, 188, set(range(7)))
            rbc = M(f"rbc{su}", [128, TC], F32, 190, set(range(7)))
            r_mean = M(f"rmean{su}", [1, TC], F32, 96, {0, 6})
            r_e2 = M(f"re2{su}", [1, TC], F32, 98, {0, 6})
            r_tmp = M(f"rtmp{su}", [1, TC], F32, 100, {0, 6})
            r_rstd = M(f"rrstd{su}", [1, TC], F32, 102, {0, 6})
            scr2 = M(f"scr2{su}", [128, TC], F32, 104, {1, 2})
            scr3 = M(f"scr3{su}", [128, TC], F32, 106, {1, 2})

            xstg = M(f"xstg{su}", [128, ND * TC // 2], F32, 80, {0})
            ln1b = M(f"ln1b{su}", [128, ND * TC], BF16, 128, {0, 1})
            xbfA = M(f"xbfA{su}", [128, ND * TC], BF16, 136, {0})
            ipW = M(f"ipW{su}", [128, 8 * 4096], BF16, 0, {0, 1})
            ips = ipW.split(8)

            for half_i in range(2):
                for tb in range(2):
                    tb2 = half_i * 2 + tb
                    dma("SP", xstg.h[:, tb * D:(tb + 1) * D],
                        in_x[t0 + tb2 * 128: t0 + (tb2 + 1) * 128, :],
                        wb=(xstg.b,))
                for tb in range(2):
                    tb2 = half_i * 2 + tb
                    for ft in range(ND):
                        par = (tb * ND + ft) % 2
                        petr(ptr[par].h[:, :], ptr[par].b,
                             xstg.h[:, tb * D + ft * 128: tb * D + (ft + 1) * 128],
                             xstg.b)
                        vcopy(x_fm.h[:, ft * TC + tb2 * 128:
                                     ft * TC + (tb2 + 1) * 128],
                              x_fm.b, ptr[par].h[:, :], ptr[par].b)
            for kc in range(8):
                P.op("SP", lambda e, o=ipW.h[:, kc * 4096:(kc + 1) * 4096],
                     i=sc_ip[0:2 * D2, kc * 128:(kc + 1) * 128]:
                     e.dma_start(out=o, in_=i, transpose=True),
                     w=(ips[kc],), r=(b_scip,), dma=True, dq="SPtr")

            def emit_ln(srcA, outb_A, xbf_half, nb):
                for hi in range(2):
                    for ft in range(nb // 2):
                        f2 = hi * (nb // 2) + ft
                        sl = srcA.h[:, f2 * TC:(f2 + 1) * TC]
                        acopy(xbf_half.h[:, ft * TC:(ft + 1) * TC], xbf_half.b,
                              sl, srcA.b)
                        act(xbf_half.h[:, (nb // 2 + ft) * TC:
                                       (nb // 2 + ft + 1) * TC],
                            xbf_half.b, sl, srcA.b, AF.Square)
                    for ft in range(nb // 2):
                        mm(p[0].h[0:1, :], p[0].b, s_onesb.h[:, :], s_onesb.b,
                           xbf_half.h[:, ft * TC:(ft + 1) * TC], xbf_half.b,
                           hi == 0 and ft == 0, hi == 1 and ft == nb // 2 - 1)
                        mm(p[1].h[0:1, :], p[1].b, s_onesb.h[:, :], s_onesb.b,
                           xbf_half.h[:, (nb // 2 + ft) * TC:
                                      (nb // 2 + ft + 1) * TC], xbf_half.b,
                           hi == 0 and ft == 0, hi == 1 and ft == nb // 2 - 1)
                ts(r_mean.h[:, :], r_mean.b, p[0].h[0:1, :], p[0].b,
                   1.0 / (nb * 128), ALU.mult)
                ts(r_e2.h[:, :], r_e2.b, p[1].h[0:1, :], p[1].b,
                   1.0 / (nb * 128), ALU.mult)
                tt(r_tmp.h[:, :], r_tmp.b, r_mean.h[:, :], r_mean.b,
                   r_mean.h[:, :], r_mean.b, ALU.mult)
                tt(r_e2.h[:, :], r_e2.b, r_e2.h[:, :], r_e2.b,
                   r_tmp.h[:, :], r_tmp.b, ALU.subtract)
                act(r_tmp.h[:, :], r_tmp.b, r_e2.h[:, :], r_e2.b, AF.Ln,
                    bias=s_eps.h[0:1, :], rextra=(s_eps.b,))
                act(r_rstd.h[:, :], r_rstd.b, r_tmp.h[:, :], r_tmp.b, AF.Exp,
                    scale=-0.5)
                mm(p[2].h[:, :], p[2].b, s_ones1.h[:, :], s_ones1.b,
                   r_mean.h[:, :], r_mean.b, True, True)
                mm(p[3].h[:, :], p[3].b, s_ones1.h[:, :], s_ones1.b,
                   r_rstd.h[:, :], r_rstd.b, True, True)
                acopy(mbc.h[:, :], mbc.b, p[2].h[:, :], p[2].b)
                acopy(rbc.h[:, :], rbc.b, p[3].h[:, :], p[3].b)
                for ft in range(nb):
                    sl = srcA.h[:, ft * TC:(ft + 1) * TC]
                    tt(scr0.h[:, :], scr0.b, sl, srcA.b, mbc.h[:, :], mbc.b,
                       ALU.subtract)
                    tt(outb_A.h[:, ft * TC:(ft + 1) * TC], outb_A.b,
                       scr0.h[:, :], scr0.b, rbc.h[:, :], rbc.b, ALU.mult)

            emit_ln(x_fm, ln1b, xbfA, ND)
            if debug and tc == 0:
                dma("GP", dbg["dbg_xfm"], x_fm.h[:, :], rb=(x_fm.b,))
                dma("GP", dbg["dbg_ln1b"], ln1b.h[:, :], rb=(ln1b.b,))

            siluzb = M(f"siluzb{su}", [128, NDT * TC], BF16, 112, {1, 2, 3})
            xp2b = M(f"xp2b{su}", [128, NDT * TC], BF16, 136, {1, 2, 3})
            xpt = [M(f"xpt{i}{su}", [128, 516], F32, 152 + i * 2.0625, {1})
                   for i in range(2)]
            for ft in range(32):
                pb = p[ft % 2]
                for kc in range(8):
                    mm(pb.h[:, :], pb.b,
                       ipW.h[:, kc * 4096 + ft * 128: kc * 4096 + (ft + 1) * 128],
                       ips[kc], ln1b.h[:, kc * TC:(kc + 1) * TC], ln1b.b,
                       kc == 0, kc == 7)
                if ft < NDT:
                    dtile = ft
                    par = dtile % 2
                    xt = xpt[par]
                    sA = [scr0, scr2][par]
                    sB = [scr1, scr3][par]
                    vcopy(xt.h[:, 0:2], xt.b,
                          s_stash.h[:, dtile * 2: dtile * 2 + 2],
                          stash_bufs[dtile])
                    acopy(xt.h[:, 2:514], xt.b, pb.h[:, :], pb.b)
                    vcopy(s_stash.h[:, dtile * 2: dtile * 2 + 2],
                          stash_bufs[dtile], xt.h[:, 512:514], xt.b)
                    k0 = s_cw.h[:, dtile * 3 + 0: dtile * 3 + 1]
                    k1 = s_cw.h[:, dtile * 3 + 1: dtile * 3 + 2]
                    k2 = s_cw.h[:, dtile * 3 + 2: dtile * 3 + 3]
                    ts(sA.h[:, :], sA.b, xt.h[:, 0:512], xt.b, k0, ALU.mult,
                       rextra=(s_cw.b,))
                    stt(sB.h[:, :], sB.b, xt.h[:, 1:513], xt.b, k1,
                        sA.h[:, :], sA.b, ALU.mult, ALU.add, rextra=(s_cw.b,))
                    stt(sA.h[:, :], sA.b, xt.h[:, 2:514], xt.b, k2,
                        sB.h[:, :], sB.b, ALU.mult, ALU.add, rextra=(s_cw.b,))
                    act(xp2b.h[:, dtile * TC:(dtile + 1) * TC], xp2b.b,
                        sA.h[:, :], sA.b, AF.Silu,
                        bias=s_cb.h[:, dtile:dtile + 1], rextra=(s_cb.b,))
                else:
                    zt = ft - NDT
                    act(siluzb.h[:, zt * TC:(zt + 1) * TC], siluzb.b,
                        pb.h[:, :], pb.b, AF.Silu)
            if debug and tc == 0:
                dma("GP", dbg["dbg_xp2b"], xp2b.h[:, :], rb=(xp2b.b,))
                dma("GP", dbg["dbg_siluz"], siluzb.h[:, :], rb=(siluzb.b,))

            dtb16 = M(f"dtb16{su}", [128, NDT * TC], BF16, 32, {2, 3})
            dtb16_s = dtb16.split(NDT)
            dtinb = M(f"dtinb{su}", [RK, TC], BF16, 98, {2})
            bcs = M(f"bcs{su}", [32, TC], BF16, 99, {2, 3})
            dtWc = M(f"dtWc{su}", [128, D2], BF16, 88, {2})
            if tc == 0:
                for kc in range(NDT):
                    dma("SP", s_xpW.h[:, kc * 96:(kc + 1) * 96],
                        sc_xp[0:96, kc * 128:(kc + 1) * 128], wb=(s_xpW.b,),
                        rb=(b_scxp,), tr=True)
            dma("SP", dtWc.h[:, :], sc_dt[0:D2, 0:128], wb=(dtWc.b,), rb=(b_scdt,), tr=True)
            if tc == 0:
                dstg = [M(f"dstg{i}", [128, 2048], BF16, 176 + i * 4,
                          {2, 3, 3.5}) for i in range(2)]
                dstages = [(dstg[0].h[:, :], dstg[0].b),
                           (dstg[1].h[:, :], dstg[1].b)]

                def _prep_pairs():
                    gp = [0]
                    scols = 2048
                    for (srcw, dstw, R, C, dstb) in prep[2:]:
                        nb = R // 128
                        for c0 in range(0, C, scols):
                            cc = min(scols, C - c0)
                            gmax = max(1, scols // C) if C <= scols else 1
                            b0 = 0
                            while b0 < nb:
                                g = min(gmax, nb - b0)
                                sap, half = dstages[gp[0]]

                                def emit(srcw=srcw, dstw=dstw, b0=b0, g=g,
                                         c0=c0, cc=cc, sap=sap, half=half,
                                         dstb=dstb):
                                    sv = sap[:, 0:g * cc].rearrange(
                                        "p (b c) -> p b c", b=g)
                                    dma("GP", sv, srcw.rearrange(
                                        "(b p) c -> p b c", p=128)
                                        [:, b0:b0 + g, c0:c0 + cc], wb=(half,))
                                    dma("ACT", dstw.rearrange(
                                        "(b p) c -> p b c", p=128)
                                        [:, b0:b0 + g, c0:c0 + cc], sv,
                                        rb=(half,), wb=(dstb,))
                                yield emit
                                gp[0] ^= 1
                                b0 += g
                            if C <= scols:
                                break

                prep_iter = _prep_pairs()
            else:
                prep_iter = iter(())
            for kc in range(NDT):
                mm(p[5].h[0:96, :], p[5].b, s_xpW.h[:, kc * 96:(kc + 1) * 96],
                   s_xpW.b, xp2b.h[:, kc * TC:(kc + 1) * TC], xp2b.b,
                   kc == 0, kc == NDT - 1)
            acopy(dtinb.h[:, :], dtinb.b, p[5].h[0:RK, :], p[5].b)
            acopy(bcs.h[:, :], bcs.b, p[5].h[RK:96, :], p[5].b)
            for dtile in range(NDT):
                pb = p[dtile % 2]
                mm(pb.h[:, :], pb.b,
                   dtWc.h[0:RK, dtile * 128:(dtile + 1) * 128], dtWc.b,
                   dtinb.h[:, :], dtinb.b, True, True)
                par = dtile % 2
                sA = [scr0, scr2][par]
                sB = [scr1, scr3][par]
                ts(sA.h[:, :], sA.b, pb.h[:, :], pb.b,
                   s_dtbv.h[:, dtile:dtile + 1], ALU.add, rextra=(s_dtbv.b,))
                act(sB.h[:, :], sB.b, sA.h[:, :], sA.b, AF.Abs)
                act(sB.h[:, :], sB.b, sB.h[:, :], sB.b, AF.Exp, scale=-1.0)
                act(sB.h[:, :], sB.b, sB.h[:, :], sB.b, AF.Ln,
                    bias=s_one.h[:, :], rextra=(s_one.b,))
                stt(dtb16.h[:, dtile * TC:(dtile + 1) * TC], dtb16_s[dtile],
                    sA.h[:, :], sA.b, 0.0, sB.h[:, :], sB.b, ALU.max, ALU.add)
            if debug and tc == 0:
                dma("GP", dbg["dbg_dtb"], dtb16.h[:, :], rb=(dtb16.b,))
                dma("GP", dbg["dbg_dtinb"], dtinb.h[:, :], rb=(dtinb.b,))
                dma("GP", dbg["dbg_bcs"], bcs.h[:, :], rb=(bcs.b,))

            Bbc = M(f"Bbc{su}", [128, NS * TC], BF16, 0, {3})
            Bbc_s = Bbc.split(NS)
            Cbc = M(f"Cbc{su}", [128, NS * TC], BF16, 16, {3})
            Cbc_s = Cbc.split(NS)
            yf = M(f"yf{su}", [128, NDT * TC], BF16, 48, {3, 4})
            s_ohD = M(f"soh{su}", [32, 32 * 128], BF16, 80, {3})
            DEPTH = 4
            a_t = [M(f"a{i}{su}", [128, TC], F32, 80 + i * 2, {3.5})
                   for i in range(DEPTH)]
            b_t = [M(f"b{i}{su}", [128, TC], BF16, 88 + i * 1, {3.5})
                   for i in range(DEPTH)]
            h_t = [M(f"h{i}{su}", [128, TC], F32, 168 + i * 2, {3.5})
                   for i in range(DEPTH)]
            chF = M(f"chF{su}", [128, DEPTH * TC], BF16, 184, {3.5})
            chF_s = chF.split(DEPTH)
            hbF = M(f"hbF{su}", [128, DEPTH * TC], BF16, 108, {3.5})
            hbF_s = hbF.split(DEPTH)
            u_t = [M(f"u{i}{su}", [128, TC], BF16, [96, 102][i], {3.5})
                     for i in range(2)]
            xpd_t = [M(f"xpd{i}{su}", [128, TC], BF16, 100 + i, {3.5})
                     for i in range(2)]
            dma("SP", s_ohD.h[:, :], c_oh, wb=(s_ohD.b,))
            for n in range(NS):
                mm(p[2].h[:, :], p[2].b, s_ohD.h[:, n * 128:(n + 1) * 128],
                   s_ohD.b, bcs.h[:, :], bcs.b, True, True)
                acopy(Bbc.h[:, n * TC:(n + 1) * TC], Bbc_s[n], p[2].h[:, :],
                      p[2].b)
                mm(p[3].h[:, :], p[3].b,
                   s_ohD.h[:, (NS + n) * 128:(NS + n + 1) * 128], s_ohD.b,
                   bcs.h[:, :], bcs.b, True, True)
                acopy(Cbc.h[:, n * TC:(n + 1) * TC], Cbc_s[n], p[3].h[:, :],
                      p[3].b)
            pend_save = [None]
            pend_ch = []

            def emit_save():
                if pend_save[0] is not None:
                    (sidx, spar) = pend_save[0]
                    acopy(s_state.h[:, sidx:sidx + 1], state_bufs[sidx],
                          h_t[spar].h[:, TC - 1:TC], h_t[spar].b)
                    acopy(hbF.h[:, spar * TC:(spar + 1) * TC], hbF_s[spar],
                          h_t[spar].h[:, :], h_t[spar].b)
                    pend_save[0] = None

            def emit_ch(upto):
                while len(pend_ch) >= 2 and pend_ch[1][0] <= upto:
                    (ci0, cp0, cn0, cpyp) = pend_ch.pop(0)
                    (ci1, cp1, cn1, _) = pend_ch.pop(0)
                    assert cp1 == cp0 + 1 and cn1 == cn0 + 1 and ci0 % 2 == 0
                    P.op("DVE", lambda e, o=chF.h[:, cp0 * TC:(cp0 + 2) * TC],
                         i0=hbF.h[:, cp0 * TC:(cp0 + 2) * TC],
                         i1=Cbc.h[:, cn0 * TC:(cn0 + 2) * TC]:
                         e.tensor_tensor(out=o, in0=i0, in1=i1, op=ALU.mult),
                         r=(hbF_s[cp0], hbF_s[cp1], Cbc_s[cn0], Cbc_s[cn1]),
                         w=(chF_s[cp0], chF_s[cp1]))
                    mm(cpyp.h[:, :], cpyp.b, s_idb.h[:, :], s_idb.b,
                       chF.h[:, cp0 * TC:(cp0 + 1) * TC], chF_s[cp0],
                       False, False)
                    mm(cpyp.h[:, :], cpyp.b, s_idb.h[:, :], s_idb.b,
                       chF.h[:, cp1 * TC:(cp1 + 1) * TC], chF_s[cp1],
                       False, cn1 == NS - 1)

            for dtile in range(NDT):
                dsl = slice(dtile * TC, (dtile + 1) * TC)
                dpar = dtile % 2
                pyp = p[4 + dpar]
                tt(u_t[dpar].h[:, :], u_t[dpar].b, dtb16.h[:, dsl],
                   dtb16_s[dtile], xp2b.h[:, dsl], xp2b.b, ALU.mult)
                act(xpd_t[dpar].h[:, :], xpd_t[dpar].b, xp2b.h[:, dsl],
                    xp2b.b, AF.Copy, scale=s_dp.h[:, dtile:dtile + 1],
                    rextra=(s_dp.b,))
                mm(pyp.h[:, :], pyp.b, s_idb.h[:, :], s_idb.b,
                   xpd_t[dpar].h[:, :], xpd_t[dpar].b, True, False)
                for n in range(NS):
                    idx = dtile * NS + n
                    par = idx % DEPTH
                    act(a_t[par].h[:, :], a_t[par].b, dtb16.h[:, dsl],
                        dtb16_s[dtile], AF.Exp, scale=s_an.h[:, idx:idx + 1],
                        rextra=(s_an.b,))
                    tt(b_t[par].h[:, :], b_t[par].b, u_t[dpar].h[:, :],
                       u_t[dpar].b, Bbc.h[:, n * TC:(n + 1) * TC], Bbc_s[n],
                       ALU.mult)
                    scan(h_t[par].h[:, :], h_t[par].b, a_t[par].h[:, :],
                         a_t[par].b, b_t[par].h[:, :], b_t[par].b,
                         s_state.h[:, idx:idx + 1], state_bufs[idx])
                    emit_save()
                    pend_save[0] = (idx, par)
                    nxt = next(prep_iter, None)
                    if nxt is not None:
                        nxt()
                    pend_ch.append((idx, par, n, pyp))
                    emit_ch(idx - 2)
                emit_ch(dtile * NS + NS - 1)
                tt(yf.h[:, dsl], yf.b, pyp.h[:, :], pyp.b,
                   siluzb.h[:, dsl], siluzb.b, ALU.mult)
            emit_save()
            if debug and tc == 0:
                dma("GP", dbg["dbg_yf"], yf.h[:, :], rb=(yf.b,))
                dma("GP", dbg["dbg_state"], s_state.h[:, :], rb=(s_state.b,))

            opW = M(f"opW{su}", [128, NDT * 1024], BF16, 0, {4})
            ops_ = opW.split(NDT)
            t1f = M(f"t1f{su}", [128, ND * TC], F32, 96, {4, 5})
            t1b = M(f"t1b{su}", [128, ND * TC], BF16, 128, {4, 5})
            for kc in range(NDT):
                P.op("SP", lambda e, o=opW.h[:, kc * 1024:(kc + 1) * 1024],
                     i=sc_op[0:D, kc * 128:(kc + 1) * 128]:
                     e.dma_start(out=o, in_=i, transpose=True),
                     w=(ops_[kc],), r=(b_scop,), dma=True, dq="SPtr")
            for ft in range(ND):
                pb = p[ft % 2]
                for kc in range(NDT):
                    mm(pb.h[:, :], pb.b,
                       opW.h[:, kc * 1024 + ft * 128: kc * 1024 + (ft + 1) * 128],
                       ops_[kc], yf.h[:, kc * TC:(kc + 1) * TC], yf.b,
                       kc == 0, kc == NDT - 1)
                par = ft % 2
                fsl = slice(ft * TC, (ft + 1) * TC)
                sA = [scr0, scr1][par]
                tt(sA.h[:, :], sA.b, x_fm.h[:, fsl], x_fm.b,
                   mbc.h[:, :], mbc.b, ALU.subtract)
                tt(sA.h[:, :], sA.b, sA.h[:, :], sA.b, rbc.h[:, :], rbc.b,
                   ALU.mult)
                tt(t1f.h[:, fsl], t1f.b, pb.h[:, :], pb.b, sA.h[:, :], sA.b,
                   ALU.add)
                acopy(t1b.h[:, fsl], t1b.b, t1f.h[:, fsl], t1f.b)
            if debug and tc == 0:
                dma("GP", dbg["dbg_t1f"], t1f.h[:, :], rb=(t1f.b,))

            w1W = M(f"w1W{su}", [128, 8 * 4096], BF16, 0, {5})
            w1s = w1W.split(8)
            hgelA = M(f"hgelA{su}", [128, NF * TC // 2], BF16, 112, {5})
            hgelB = M(f"hgelB{su}", [128, NF * TC // 2], BF16, 80, {5})
            x1 = M(f"x1{su}", [128, ND * TC], F32, 152, {5, 6, 7, 8})
            for kc in range(8):
                P.op("SP", lambda e, o=w1W.h[:, kc * 4096:(kc + 1) * 4096],
                     i=sc_w1[0:DF, kc * 128:(kc + 1) * 128]:
                     e.dma_start(out=o, in_=i, transpose=True),
                     w=(w1s[kc],), r=(b_scw1,), dma=True, dq="SPtr")

            def hgel_ap(kb):
                if kb < 16:
                    return hgelA.h[:, kb * TC:(kb + 1) * TC], hgelA.b
                return hgelB.h[:, (kb - 16) * TC:(kb - 15) * TC], hgelB.b

            for ft in range(NF):
                pb = p[ft % 2]
                for kc in range(8):
                    mm(pb.h[:, :], pb.b,
                       w1W.h[:, kc * 4096 + ft * 128: kc * 4096 + (ft + 1) * 128],
                       w1s[kc], t1b.h[:, kc * TC:(kc + 1) * TC], t1b.b,
                       kc == 0, kc == 7)
                ga, gb = hgel_ap(ft)
                act(ga, gb, pb.h[:, :], pb.b, AF.Gelu,
                    bias=s_b1.h[:, ft:ft + 1], rextra=(s_b1.b,))
            for kb in range(32):
                P.op("SP", lambda e, o=w1W.h[:, kb * 1024:(kb + 1) * 1024],
                     i=sc_w2[0:D, kb * 128:(kb + 1) * 128]:
                     e.dma_start(out=o, in_=i, transpose=True),
                     w=(w1s[kb // 4],), r=(b_scw2,), dma=True, dq="SPtr")
            for ft in range(ND):
                pb = p[2 + ft % 2]
                for kb in range(32):
                    ga, gb = hgel_ap(kb)
                    mm(pb.h[:, :], pb.b,
                       w1W.h[:, kb * 1024 + ft * 128: kb * 1024 + (ft + 1) * 128],
                       w1s[kb // 4], ga, gb, kb == 0, kb == 31)
                par = ft % 2
                fsl = slice(ft * TC, (ft + 1) * TC)
                sA = [scr0, scr1][par]
                stt(sA.h[:, :], sA.b, pb.h[:, :], pb.b,
                    s_b2.h[:, ft:ft + 1], t1f.h[:, fsl], t1f.b,
                    ALU.add, ALU.add, rextra=(s_b2.b,))
                tt(x1.h[:, fsl], x1.b, sA.h[:, :], sA.b, x_fm.h[:, fsl], x_fm.b,
                   ALU.add)
            if debug and tc == 0:
                dma("GP", dbg["dbg_x1"], x1.h[:, :], rb=(x1.b,))

            ln2b = M(f"ln2b{su}", [128, ND * TC], BF16, 128, {6, 7})
            xbfG = M(f"xbfG{su}", [128, ND * TC], BF16, 136, {6})
            emit_ln(x1, ln2b, xbfG, ND)
            if debug and tc == 0:
                dma("GP", dbg["dbg_ln2b"], ln2b.h[:, :], rb=(ln2b.b,))

            wgW = M(f"wgW{su}", [128, 8 * H2], BF16, 0, {7})
            wgs = wgW.split(8)
            wuW = M(f"wuW{su}", [128, 8 * H2], BF16, 48, {7})
            wus = wuW.split(8)
            hs = M(f"hs{su}", [128, NH * TC], BF16, 96, {7, 8})
            for kc in range(8):
                P.op("SP", lambda e, o=wgW.h[:, kc * H2:(kc + 1) * H2],
                     i=sc_wg[0:H2, kc * 128:(kc + 1) * 128]:
                     e.dma_start(out=o, in_=i, transpose=True),
                     w=(wgs[kc],), r=(b_scwg,), dma=True, dq="SPtr")
                P.op("SP", lambda e, o=wuW.h[:, kc * H2:(kc + 1) * H2],
                     i=sc_wu[0:H2, kc * 128:(kc + 1) * 128]:
                     e.dma_start(out=o, in_=i, transpose=True),
                     w=(wus[kc],), r=(b_scwu,), dma=True, dq="SPtr")
            for ft in range(NH):
                par = ft % 2
                pg = p[par]
                pu = p[2 + par]
                for kc in range(8):
                    mm(pg.h[:, :], pg.b,
                       wgW.h[:, kc * H2 + ft * 128: kc * H2 + (ft + 1) * 128],
                       wgs[kc], ln2b.h[:, kc * TC:(kc + 1) * TC], ln2b.b,
                       kc == 0, kc == 7)
                for kc in range(8):
                    mm(pu.h[:, :], pu.b,
                       wuW.h[:, kc * H2 + ft * 128: kc * H2 + (ft + 1) * 128],
                       wus[kc], ln2b.h[:, kc * TC:(kc + 1) * TC], ln2b.b,
                       kc == 0, kc == 7)
                sA = [scr0, scr1][par]
                act(sA.h[:, :], sA.b, pg.h[:, :], pg.b, AF.Silu, scale=QSCALE)
                tt(hs.h[:, ft * TC:(ft + 1) * TC], hs.b, sA.h[:, :], sA.b,
                   pu.h[:, :], pu.b, ALU.mult)
            if debug and tc == 0:
                dma("GP", dbg["dbg_hs"], hs.h[:, :], rb=(hs.b,))

            wdW = M(f"wdW{su}", [128, NH * 1024], BF16, 0, {8})
            wds = wdW.split(NH)
            outF = M(f"outF{su}", [128, ND * TC], F32, 64, {8})
            outtm = M(f"outtm{su}", [128, 4 * D], F32, 80, {8})
            for kb in range(NH):
                P.op("SP", lambda e, o=wdW.h[:, kb * 1024:(kb + 1) * 1024],
                     i=sc_wd[0:D, kb * 128:(kb + 1) * 128]:
                     e.dma_start(out=o, in_=i, transpose=True),
                     w=(wds[kb],), r=(b_scwd,), dma=True, dq="SPtr")
            for ft in range(ND):
                pb = p[ft % 2]
                for kb in range(NH):
                    mm(pb.h[:, :], pb.b,
                       wdW.h[:, kb * 1024 + ft * 128: kb * 1024 + (ft + 1) * 128],
                       wds[kb], hs.h[:, kb * TC:(kb + 1) * TC], hs.b,
                       kb == 0, kb == NH - 1)
                fsl = slice(ft * TC, (ft + 1) * TC)
                stt(outF.h[:, fsl], outF.b, pb.h[:, :], pb.b,
                    QSCALE * QSCALE, x1.h[:, fsl], x1.b, ALU.mult, ALU.add)
            for tb in range(4):
                for ft in range(ND):
                    par = (tb * ND + ft) % 2
                    petr(ptr[par].h[:, :], ptr[par].b,
                         outF.h[:, ft * TC + tb * 128: ft * TC + (tb + 1) * 128],
                         outF.b)
                    vcopy(outtm.h[:, tb * D + ft * 128: tb * D + (ft + 1) * 128],
                          outtm.b, ptr[par].h[:, :], ptr[par].b)
            for tb in range(4):
                dma("GP", out_y[t0 + tb * 128: t0 + (tb + 1) * 128, :],
                    outtm.h[:, tb * D:(tb + 1) * D], rb=(outtm.b,))

        P.barrier()
        P.assign_values()
        with nc.Block() as blk:
            blk.sync(lambda e: P.emit_engine("SP", e, sems, sbar))
            blk.tensor(lambda e: P.emit_engine("PE", e, sems, sbar))
            blk.scalar(lambda e: P.emit_engine("ACT", e, sems, sbar))
            blk.vector(lambda e: P.emit_engine("DVE", e, sems, sbar))
            blk.gpsimd(lambda e: P.emit_engine("GP", e, sems, sbar))

    return nc


def _consts():
    oh = np.zeros((32, 32 * 128), np.float32)
    for j in range(32):
        oh[j, j * 128:(j + 1) * 128] = 1.0
    return {
        "c_idf": np.eye(128, dtype=np.float32),
        "c_idb": np.eye(128, dtype=np.float32).astype(ml_dtypes.bfloat16),
        "c_onesb": np.ones((128, 1), np.float32).astype(ml_dtypes.bfloat16),
        "c_ones1": np.ones((1, 128), np.float32),
        "c_oh": oh.astype(ml_dtypes.bfloat16),
    }


_NAMES = [
    ("w_ip", "L0_attn_in_proj_w", np.float32),
    ("w_cw", "L0_attn_conv_w", np.float32),
    ("w_cb", "L0_attn_conv_b", np.float32),
    ("w_xp", "L0_attn_x_proj_w", np.float32),
    ("w_dt", "L0_attn_dt_proj_w", np.float32),
    ("w_dtb", "L0_attn_dt_proj_b", np.float32),
    ("w_al", "L0_attn_A_log", np.float32),
    ("w_dp", "L0_attn_Dp", np.float32),
    ("w_op", "L0_attn_out_proj_w", np.float32),
    ("w_w1", "L0_attn_w1", np.float32),
    ("w_b1", "L0_attn_b1", np.float32),
    ("w_w2", "L0_attn_w2", np.float32),
    ("w_b2", "L0_attn_b2", np.float32),
    ("w_wg", "L0_ffn_w_gate", np.int32),
    ("w_wu", "L0_ffn_w_up", np.int32),
    ("w_wd", "L0_ffn_w_down", np.int32),
]


def kernel(**inputs):
    nc = build_nc()
    base = dict(_consts())
    for (k, src, dt) in _NAMES:
        base[k] = np.ascontiguousarray(np.asarray(inputs[src], dt))
    x = np.ascontiguousarray(np.asarray(inputs["L0_attn_x"], np.float32))
    in_maps = [dict(base, in_x=np.ascontiguousarray(x[b])) for b in range(8)]
    res = run_bass_kernel_spmd(nc, in_maps, core_ids=list(range(8)))
    return np.stack(
        [np.asarray(res.results[b]["out_y"], np.float32) for b in range(8)],
        axis=0)
